# version 12
# baseline (speedup 1.0000x reference)
import numpy as np
import ml_dtypes
import concourse.bass as bass
import concourse.mybir as mybir
from concourse.bass_utils import run_bass_kernel_spmd

F32 = mybir.dt.float32
BF16 = mybir.dt.bfloat16
AF = mybir.ActivationFunctionType
ALU = mybir.AluOpType
AX = mybir.AxisListType
BF = ml_dtypes.bfloat16

H, DH, DIM = 32, 64, 2048
T, E = 2048, 128
RN = T + E
NF = DIM // 128
NP = H // 2
SCALE = DH ** -0.5
NEG = -1e30
EPS_LN = 1e-5
EPS_RMS = float(np.finfo(np.float32).eps)
INTER = 5632
NI = INTER // 128
CBS = 64
NBLK = T // CBS
NCK = 68
NKK = 256


ENGINES = ("sync", "tensor", "vector", "scalar", "gpsimd")


class Prog:

    def __init__(self):
        self.ops = {e: [] for e in ENGINES}
        self.cnt = {}
        self.pending = {}

    def set_barrier(self, tokens):
        for e in ENGINES:
            self.pending[e] = list(tokens)

    def engine_totals(self):
        return [(s, v, "n", True) for s, v in self.cnt.items()
                if s in ("pe", "dve", "act", "pool")]

    def emit(self, eng, fn, waits=(), inc=None, amt=1):
        wl = []
        waits = list(waits) + self.pending.pop(eng, [])
        for w in waits:
            if w is None:
                continue
            sem, val, peng, asy = w
            if peng == eng and not asy:
                continue
            wl.append((sem, val))
        tok = None
        if inc is not None:
            self.cnt[inc] = self.cnt.get(inc, 0) + amt
            tok = (inc, self.cnt[inc], eng, eng in ("sync", "gpsimd") and amt % 16 == 0)
        self.ops[eng].append((fn, wl, inc, amt))
        return tok

    def dma(self, queue, sem, out, in_, waits=(), **kw):
        return self.emit(
            queue,
            lambda e: e.dma_start(out=out, in_=in_, **kw),
            waits=waits, inc=sem, amt=16)

    def replay(self, block, sems):
        for eng in ENGINES:
            section = self.ops[eng]
            if not section:
                continue

            def run(engine, section=section):
                seen = {}
                for fn, waits, inc, amt in section:
                    for s, v in waits:
                        if seen.get(s, -1) >= v:
                            continue
                        engine.wait_ge(sems[s], v)
                        seen[s] = v
                    ins = fn(engine)
                    if inc is not None:
                        ins.then_inc(sems[inc], amt)
            getattr(block, eng)(run)
        self.ops = {e: [] for e in ENGINES}


class Rot:

    def __init__(self, n):
        self.n = n
        self.tok = [[] for _ in range(n)]
        self.i = 0

    def acquire(self):
        s = self.i % self.n
        self.i += 1
        waits = self.tok[s]
        self.tok[s] = []
        return s, waits

    def release(self, s, tok):
        if tok is not None:
            self.tok[s].append(tok)


def build_nc(debug=0):
    nc = bass.Bass("TRN2", target_bir_lowering=False,
                   detect_race_conditions=False)
    p = Prog()

    def din(name, shape, dtype=F32):
        return nc.dram_tensor(name, list(shape), dtype, kind="ExternalInput").ap()

    x_in = din("x_in", (RN, DIM))
    wqkv = din("wqkv", (DIM, 3 * DIM))
    rmsw_c = din("rmsw_c", (128, NF))
    wcomb = din("wcomb", (DIM, 96))
    bcomb = din("bcomb", (1, 96))
    kcw_l = din("kcw_l", (NP, 2, 128, CBS, 64))
    kpos_l = din("kpos_l", (NP, 2, 128, CBS))
    kcb_l = din("kcb_l", (128, 2 * NP))
    mem_l = din("mem_l", (2, NP, 128, 4))
    wout = din("wout", (DIM, DIM))
    gateT = din("gateT", (DIM, INTER))
    upT = din("upT", (DIM, INTER))
    downT = din("downT", (INTER, DIM))
    gate_bf = nc.dram_tensor("gate_bf", [DIM, INTER], BF16, kind="Internal").ap()
    up_bf = nc.dram_tensor("up_bf", [DIM, INTER], BF16, kind="Internal").ap()
    down_bf = nc.dram_tensor("down_bf", [INTER, DIM], BF16, kind="Internal").ap()
    gs_c = din("gs_c", (128, NI))
    us_c = din("us_c", (128, NI))
    ds_c = din("ds_c", (128, NF))
    ident_h = din("ident_h", (128, 128), BF16)
    ident32_h = din("ident32_h", (128, 128), F32)
    lmask_h = din("lmask_h", (2, 128, 2 * NKK), BF16)
    cmask_h = din("cmask_h", (16, 128, 2 * NCK), BF16)

    out_h = nc.dram_tensor("out", [T, DIM], F32, kind="ExternalOutput").ap()

    kind_sp = "ExternalOutput" if debug else "Internal"
    q_sp = nc.dram_tensor("q_sp", [NF, 128, T], BF16, kind=kind_sp).ap()
    k_sp = nc.dram_tensor("k_sp", [NF, 128, RN], BF16, kind=kind_sp).ap()
    v_sp = nc.dram_tensor("v_sp", [NF + 1, 128, DIM], BF16, kind=kind_sp).ap()
    ck_own = nc.dram_tensor("ck_own", [2, NP, 128, NBLK], F32, kind="Internal").ap()
    ck_all = nc.dram_tensor("ck_all", [2, 2, NP, 128, NBLK], F32, kind="Internal").ap()
    x1_sp = nc.dram_tensor("x1_sp", [T, DIM], F32,
                           kind="ExternalOutput" if debug else "Internal").ap()
    if debug:
        xha_dbg = nc.dram_tensor("xha_dbg", [128, NF * RN], BF16, kind="ExternalOutput").ap()
        cko_dbg = nc.dram_tensor("cko_dbg", [2, NP, 128, NBLK], F32, kind="ExternalOutput").ap()
        st_dbg = nc.dram_tensor("st_dbg", [128, 64], F32, kind="ExternalOutput").ap()
        cw_dbg = nc.dram_tensor("cw_dbg", [128, 16 * 96], F32, kind="ExternalOutput").ap()
        ckt_dbg = nc.dram_tensor("ckt_dbg", [2, 128, NP * 2 * NCK], BF16, kind="ExternalOutput").ap()

    sem_names = ["qc", "qw", "qsl", "pe", "dve", "act", "cc",
                 "qx0", "qx1", "kx0", "kx1", "kc0", "kc1", "kw0", "kw1",
                 "qw0", "qw1", "cko", "qld", "qdb", "sA0", "sA1", "sB0", "sB1",
                 "sC0", "sC1", "sD0", "sD1", "out0", "out1", "pool"]

    import contextlib
    ctx = contextlib.ExitStack()
    with ctx:
        sems = {s: ctx.enter_context(nc.semaphore(s)) for s in sem_names}

        pers = contextlib.ExitStack()
        sb = lambda name, shape, dt=BF16: pers.enter_context(nc.sbuf_tensor(name, list(shape), dt))
        with pers:
            ident = sb("ident", (128, 128))
            ident32 = sb("ident32", (128, 128), F32)
            rmsw_s = sb("rmsw_s", (128, NF), F32)
            scales_g = sb("scales_g", (128, NI), F32)
            scales_u = sb("scales_u", (128, NI), F32)
            scales_d = sb("scales_d", (128, NF), F32)
            eps4_c = sb("eps4_c", (128, 1), F32)
            ph13 = contextlib.ExitStack()
            sb13 = lambda name, shape, dt=BF16: ph13.enter_context(nc.sbuf_tensor(name, list(shape), dt))
            cw_tm = sb13("cw_tm", (128, 16 * 96), F32)
            ck_sb = sb13("ck_sb", (128, NP * 2 * NCK))
            cvt_sb = sb13("cvt_sb", (128, H * 64))
            lmask = sb13("lmask", (128, 2 * 2 * NKK))
            cmask = sb13("cmask", (128, 16 * 2 * NCK))

            t_ep4 = p.emit("vector", lambda e: e.memset(eps4_c[:], EPS_LN), inc="dve")
            p.dma("sync", "qc", ident[:], ident_h[:, :])
            p.dma("sync", "qc", ident32[:], ident32_h[:, :])
            p.dma("sync", "qc", rmsw_s[:], rmsw_c[:, :])
            p.dma("sync", "qc", lmask[:].rearrange("p (s n) -> p s n", s=2),
                  lmask_h.rearrange("s p n -> p s n"))
            p.dma("sync", "qc", cmask[:].rearrange("p (s n) -> p s n", s=16),
                  cmask_h.rearrange("s p n -> p s n"))
            p.dma("sync", "qc", scales_g[:], gs_c[:, :])
            p.dma("sync", "qc", scales_u[:], us_c[:, :])
            p.dma("sync", "qc", scales_d[:], ds_c[:, :])

            ph1 = contextlib.ExitStack()
            sb1 = lambda name, shape, dt=BF16: ph1.enter_context(nc.sbuf_tensor(name, list(shape), dt))
            ps1 = lambda name, shape, dt=F32: ph1.enter_context(nc.psum_tensor(name, list(shape), dt))
            with ph1:
                xha = sb1("xha", (128, NF * RN))
                xt = [sb1(f"xt{i}", (128, DIM), F32) for i in range(2)]
                sqt = sb1("sqt", (128, DIM), F32)
                xhtm = [sb1(f"xhtm{i}", (128, DIM)) for i in range(2)]
                cols = sb1("cols", (128, 16), F32)
                ssum = sb1("ssum", (128, NF + 1), F32)
                ssq = sb1("ssq", (128, NF + 1), F32)
                stm = sb1("stm", (128, NF + 1), F32)
                stq = sb1("stq", (128, NF + 1), F32)
                stt = sb1("stt", (128, NF + 1), F32)
                stv = sb1("stv", (128, NF + 1), F32)
                str1 = sb1("str1", (128, NF + 1), F32)
                stS = sb1("stS", (128, NF + 1), F32)
                stnm = sb1("stnm", (128, NF + 1), F32)
                wslab = [sb1(f"wslab{i}", (128, DIM // 2)) for i in range(NF)]
                kx = [sb1(f"kx{i}", (128, RN + CBS)) for i in range(2)]
                kcp = [sb1(f"kcp{i}", (128, CBS * 64)) for i in range(2)]
                qstage = [sb1(f"qstage{i}", (128, 512)) for i in range(2)]
                ckst = [sb1(f"ckst{i}", (128, NBLK), F32) for i in range(2)]
                kcbs = sb1("kcbs", (128, 2 * NP), F32)
                bcomb_bc = sb1("bcomb_bc", (128, 96), F32)
                bcomb_s = sb1("bcomb_s", (1, 96))
                cwtmp = sb1("cwtmp", (128, 96), F32)
                stcap = sb1("stcap", (128, 64), F32) if debug else None

                epsln_c = sb1("epsln_c", (128, 1), F32)
                epsrm_c = sb1("epsrm_c", (128, 1), F32)
                vtst = sb1("vtst", (128, (NF + 1) * 128))
                pst = [ps1(f"pst{i}", (128, 128), BF16) for i in range(2)]
                ps_vt = ps1("ps_vt", (128, 512), F32)
                psb = [ps1(f"psb{i}", (128, 512)) for i in range(2)]
                psc0 = ps1("psc0", (64, NBLK + 1))
                psc1 = ps1("psc1", (128, NBLK + 1))
                pscb = ps1("pscb", (128, 96))

                p.dma("sync", "qc", kcbs[:], kcb_l[:, :])
                t_qc = ("qc", p.cnt["qc"], "sync", True)
                t_id = t_rw = t_kcb = t_qc
                t_ep1 = p.emit("vector", lambda e: e.memset(epsln_c[:], EPS_LN), inc="dve")
                t_ep2 = p.emit("vector", lambda e: e.memset(epsrm_c[:], EPS_RMS), inc="dve")
                p.dma("gpsimd", "qw", bcomb_s[:], bcomb[:, :])

                xrot, trot = Rot(2), Rot(2)
                NTC = NF + 1
                for tc in range(NTC):
                    s, wt = xrot.acquire()
                    t_x = p.dma("sync", f"qx{s}", xt[s][:],
                                x_in[tc * 128:(tc + 1) * 128, :], waits=wt)
                    v = "vector"
                    p.emit(v, lambda e, s=s: e.tensor_mul(sqt[:], xt[s][:], xt[s][:]),
                           waits=[t_x])
                    p.emit(v, lambda e, s=s, tc=tc: e.reduce_sum(
                        ssum[:, tc:tc + 1], xt[s][:], AX.X))
                    t_r = p.emit(v, lambda e, tc=tc: e.reduce_sum(
                        ssq[:, tc:tc + 1], sqt[:], AX.X), inc="dve")
                    xrot.release(s, t_r)
                v = "vector"
                SP = lambda: p.emit(v, lambda e: e.tensor_copy(cols[:, 15:16],
                                                               ssum[:, 0:1]))
                SP(); SP(); SP()
                p.emit(v, lambda e: e.tensor_scalar_mul(stm[:], ssum[:], 1.0 / DIM))
                p.emit(v, lambda e: e.tensor_scalar_mul(stq[:], ssq[:], 1.0 / DIM))
                SP(); SP()
                p.emit(v, lambda e: e.tensor_mul(stt[:], stm[:], stm[:]))
                SP(); SP()
                t_var = p.emit(v, lambda e: e.tensor_sub(stv[:], stq[:], stt[:]),
                               inc="dve")
                t_s1 = p.emit("scalar", lambda e: e.activation(
                    stt[:], stv[:], AF.Sqrt, bias=epsln_c[:, :]),
                    waits=[t_var, t_ep1], inc="act")
                p.emit(v, lambda e: e.reciprocal(str1[:], stt[:]), waits=[t_s1])
                SP(); SP(); SP()
                p.emit(v, lambda e: e.tensor_mul(stt[:], stv[:], str1[:]))
                SP(); SP()
                t_b2 = p.emit(v, lambda e: e.tensor_mul(stq[:], stt[:], str1[:]),
                              inc="dve")
                t_s2 = p.emit("scalar", lambda e: e.activation(
                    stt[:], stq[:], AF.Sqrt, bias=epsrm_c[:, :]),
                    waits=[t_b2, t_ep2], inc="act")
                p.emit(v, lambda e: e.reciprocal(stq[:], stt[:]), waits=[t_s2])
                SP(); SP(); SP()
                p.emit(v, lambda e: e.tensor_mul(stS[:], str1[:], stq[:]))
                SP(); SP()
                p.emit(v, lambda e: e.tensor_mul(stnm[:], stm[:], stS[:]))
                SP(); SP()
                t_stats = p.emit(v, lambda e: e.tensor_scalar_mul(stnm[:], stnm[:], -1.0),
                                 inc="dve")
                SP(); SP(); SP()
                if debug:
                    p.emit(v, lambda e: e.tensor_copy(stcap[:, 0:NTC], stm[:]))
                    p.emit(v, lambda e: e.tensor_copy(stcap[:, 32:32 + NTC], stS[:]))
                xa_done = []
                xh_reader = [None, None]
                for tc in range(NTC):
                    s, wt = xrot.acquire()
                    t_x = p.dma("sync", f"qx{s}", xt[s][:],
                                x_in[tc * 128:(tc + 1) * 128, :], waits=wt)
                    hs = tc % 2
                    t_xh = p.emit(v, lambda e, s=s, hs=hs, tc=tc: e.tensor_scalar(
                        xhtm[hs][:], xt[s][:], stS[:, tc:tc + 1], stnm[:, tc:tc + 1],
                        ALU.mult, ALU.add),
                        waits=[t_x, t_stats, xh_reader[hs]], inc="dve")
                    xrot.release(s, t_xh)
                    for f in range(NF):
                        ts, twt = trot.acquire()
                        t_tp = p.emit("tensor", lambda e, f=f, hs=hs, ts=ts: e.transpose(
                            pst[ts][:, :], xhtm[hs][:, f * 128:(f + 1) * 128],
                            ident[:, :]), waits=twt + [t_xh, t_id], inc="pe")
                        if f == NF - 1:
                            xh_reader[hs] = t_tp
                        t_ev = p.emit(v, lambda e, f=f, tc=tc, ts=ts: e.tensor_scalar_mul(
                            xha[:, f * RN + tc * 128: f * RN + (tc + 1) * 128],
                            pst[ts][:, :], rmsw_s[:, f:f + 1]),
                            waits=[t_tp, t_rw], inc="dve")
                        trot.release(ts, t_ev)
                        if f == NF - 1:
                            xa_done.append(t_ev)
                xa_all = xa_done[-1]

                if debug:
                    p.dma("gpsimd", "qdb", xha_dbg[:, :], xha[:], waits=[xa_all])
                    p.dma("gpsimd", "qdb", st_dbg[:, :], stcap[:], waits=[xa_all])

                brot = Rot(2)
                qsrot = Rot(2)
                kxrot = Rot(2)
                kcrot = Rot(2)
                ckrot = Rot(2)
                slab_tok = {}
                psc_guard = [[], []]

                def qkv_group(grp, prev_pe):
                    is_q = (grp == 0)
                    coff = E if is_q else 0
                    nch = [(i * 512, 512) for i in range(4)] + ([] if is_q else [(2048, 128)])
                    spill = [q_sp, k_sp, v_sp][grp]
                    last_pe = prev_pe
                    qtr_pe = [prev_pe, prev_pe]
                    for jh in range(4):
                      jo = jh * (NF // 4)
                      qsl_s = jh % 2
                      for kc in range(NF):
                        p.dma(
                            "gpsimd", "qsl",
                            wslab[kc][:, qsl_s * 512:(qsl_s + 1) * 512],
                            wqkv[kc * 128:(kc + 1) * 128,
                                 grp * DIM + jh * (DIM // 4):grp * DIM + (jh + 1) * (DIM // 4)],
                            waits=[qtr_pe[qsl_s]] if kc == 0 else [])
                      t_slabs = ("qsl", p.cnt["qsl"], "gpsimd", True)
                      for jr in range(NF // 4):
                        j = jo + jr
                        if not is_q:
                            ks, kwt = kcrot.acquire()
                            t_kc = p.dma("gpsimd", f"kc{ks}", kcp[ks][:].rearrange(
                                "p (t o) -> p t o", t=CBS),
                                kcw_l[j, grp - 1, :, :, :], waits=kwt)
                            xs, xwt = kxrot.acquire()
                            t_kp = p.dma("gpsimd", f"kx{xs}", kx[xs][:, RN:RN + CBS],
                                         kpos_l[j, grp - 1, :, :], waits=xwt)
                        ev_toks = []
                        for (c0, cw) in nch:
                            bs, bwt = brot.acquire()
                            mm = None
                            for kc in range(NF):
                                wts = [t_slabs, xa_all] + (bwt if kc == 0 else [])
                                tk = p.emit("tensor", lambda e, kc=kc, jr=jr, bs=bs, c0=c0, cw=cw, qsl_s=qsl_s: e.matmul(
                                    psb[bs][:, 0:cw],
                                    wslab[kc][:, qsl_s * 512 + jr * 128:
                                               qsl_s * 512 + (jr + 1) * 128],
                                    xha[:, kc * RN + coff + c0: kc * RN + coff + c0 + cw],
                                    start=(kc == 0), stop=(kc == NF - 1)),
                                    waits=wts, inc="pe" if kc == NF - 1 else None)
                                if tk is not None:
                                    mm = tk
                            last_pe = mm
                            if is_q:
                                ss, swt = qsrot.acquire()
                                t_ev = p.emit("vector", lambda e, bs=bs, ss=ss, cw=cw: e.tensor_scalar_mul(
                                    qstage[ss][:, 0:cw], psb[bs][:, 0:cw], SCALE),
                                    waits=[mm] + swt, inc="dve")
                                t_st = p.dma("gpsimd", f"qw{ss}", spill[j, :, c0:c0 + cw],
                                             qstage[ss][:, 0:cw], waits=[t_ev])
                                qsrot.release(ss, t_st)
                            else:
                                t_ev = p.emit("vector", lambda e, bs=bs, xs=xs, c0=c0, cw=cw: e.tensor_copy(
                                    kx[xs][:, c0:c0 + cw], psb[bs][:, 0:cw]),
                                    waits=[mm] + (xwt if c0 == 0 else []), inc="dve")
                                ev_toks.append(t_ev)
                            brot.release(bs, t_ev)
                        if is_q:
                            continue
                        if grp == 1:
                            t_sp = p.dma("sync", f"kw{xs}", spill[j, :, 0:RN],
                                         kx[xs][:, 0:RN], waits=ev_toks)
                        else:
                            cpy = None
                            for tcx in range(NF + 1):
                                bidx = tcx % 4
                                wt2 = (ev_toks if tcx == 0 else [])
                                if bidx == 0:
                                    wt2 = wt2 + [cpy]
                                last_g = (bidx == 3 or tcx == NF)
                                t_tp = p.emit("tensor", lambda e, tcx=tcx, bidx=bidx, xs=xs, last_g=last_g: e.matmul(
                                    ps_vt[:, bidx * 128:(bidx + 1) * 128],
                                    kx[xs][:, tcx * 128:(tcx + 1) * 128], ident[:, :],
                                    start=(bidx == 0), stop=last_g),
                                    waits=wt2 + [t_id],
                                    inc="pe" if last_g else None)
                                if last_g:
                                    nb = bidx + 1
                                    cpy = p.emit("vector", lambda e, tcx=tcx, nb=nb: e.tensor_copy(
                                        vtst[:, (tcx - nb + 1) * 128:(tcx + 1) * 128],
                                        ps_vt[:, 0:nb * 128]), waits=[t_tp], inc="dve")
                            t_sp = p.dma(
                                "sync", f"kw{xs}",
                                v_sp[:, :, j * 128:(j + 1) * 128].rearrange("w p f -> p w f"),
                                vtst[:].rearrange("p (w f) -> p w f", w=NF + 1),
                                waits=[cpy])
                            last_pe = t_tp
                        lastmm = None
                        for t in range(CBS):
                            st_, sp_ = (t == 0), (t == CBS - 1)
                            wts = ([t_kc, t_kp] + ev_toks + psc_guard[0] + psc_guard[1]) \
                                if t == 0 else []
                            p.emit("tensor", lambda e, t=t, ks=ks, xs=xs, st_=st_, sp_=sp_: e.matmul(
                                psc0[:, :], kcp[ks][0:64, t * 64:(t + 1) * 64],
                                kx[xs][0:64, E + t: E + t + 64 * NBLK + 1: 64],
                                start=st_, stop=sp_), waits=wts)
                            lastmm = p.emit("tensor", lambda e, t=t, ks=ks, xs=xs, st_=st_, sp_=sp_: e.matmul(
                                psc1[64:128, :], kcp[ks][64:128, t * 64:(t + 1) * 64],
                                kx[xs][64:128, E + t: E + t + 64 * NBLK + 1: 64],
                                start=st_, stop=sp_, tile_position=(64, 64)),
                                inc="pe" if sp_ else None)
                        last_pe = lastmm
                        kcrot.release(ks, lastmm)
                        cs, cwt = ckrot.acquire()
                        kcol = (grp - 1) * NP + j
                        t_e0 = p.emit("vector", lambda e, cs=cs, kcol=kcol: e.tensor_scalar(
                            ckst[cs][0:64, :], psc0[0:64, 0:NBLK], psc0[0:64, NBLK:NBLK + 1],
                            kcbs[0:64, kcol:kcol + 1], ALU.add, ALU.add),
                            waits=[lastmm, t_kcb] + cwt, inc="dve")
                        t_e1 = p.emit("vector", lambda e, cs=cs, kcol=kcol: e.tensor_scalar(
                            ckst[cs][64:128, :], psc1[64:128, 0:NBLK], psc1[64:128, NBLK:NBLK + 1],
                            kcbs[64:128, kcol:kcol + 1], ALU.add, ALU.add), inc="dve")
                        psc_guard[0] = [t_e0]
                        psc_guard[1] = [t_e1]
                        t_cko = p.dma("gpsimd", f"sA{cs}", ck_own[grp - 1, j, :, :], ckst[cs][:],
                                      waits=[t_e0, t_e1])
                        ckrot.release(cs, t_cko)
                        kxrot.release(xs, t_sp)
                        kxrot.release(xs, last_pe)
                    return last_pe

                g_pe = qkv_group(0, None)

                wcomb_sb = sb1("wcomb_sb", (128, NF * 96))
                p.dma("gpsimd", "qw", wcomb_sb[:].rearrange("p (k c) -> p k c", k=NF),
                      wcomb.rearrange("(k p) c -> p k c", p=128))
                t_wc = t_bc = ("qw", p.cnt["qw"], "gpsimd", True)
                ones_sb = sb1("ones_sb", (1, 128))
                t_on = p.emit("vector", lambda e: e.memset(ones_sb[:], 1.0), inc="dve")
                t_bb = p.emit("tensor", lambda e: e.matmul(
                    pscb[:, :], ones_sb[:, :], bcomb_s[:, :], start=True, stop=True),
                    waits=[t_bc, t_on], inc="pe")
                t_bbe = p.emit("vector", lambda e: e.tensor_copy(bcomb_bc[:], pscb[:]),
                               waits=[t_bb], inc="dve")
                prev_ad = t_bbe
                prev_sig = None
                for tc in range(16):
                    mm = None
                    for kc in range(NF):
                        tk = p.emit("tensor", lambda e, tc=tc, kc=kc: e.matmul(
                            pscb[:, :],
                            xha[:, kc * RN + E + tc * 128: kc * RN + E + (tc + 1) * 128],
                            wcomb_sb[:, kc * 96:(kc + 1) * 96],
                            start=(kc == 0), stop=(kc == NF - 1)),
                            waits=[prev_ad, t_wc, xa_all] if kc == 0 else [],
                            inc="pe" if kc == NF - 1 else None)
                        if tk is not None:
                            mm = tk
                    t_ad = p.emit("vector", lambda e, tc=tc: e.tensor_add(
                        cwtmp[:], pscb[:], bcomb_bc[:]), waits=[mm, prev_sig], inc="dve")
                    prev_ad = t_ad
                    prev_sig = p.emit("scalar", lambda e, tc=tc: e.activation(
                        cw_tm[:, tc * 96:(tc + 1) * 96], cwtmp[:], AF.Sigmoid),
                        waits=[t_ad], inc="act")

                g_pe = qkv_group(1, g_pe)
                g_pe = qkv_group(2, g_pe)

                all_ck = [("sA0", p.cnt.get("sA0", 0), "gpsimd", True),
                          ("sA1", p.cnt.get("sA1", 0), "gpsimd", True)]
                t_cc = p.emit("gpsimd", lambda e: e.collective_compute(
                    "AllGather", ALU.bypass,
                    replica_groups=[[0, 1], [2, 3], [4, 5], [6, 7]],
                    ins=[ck_own.rearrange("kv g p n -> (kv g p) n")],
                    outs=[ck_all.rearrange("r kv g p n -> (r kv g p) n")]),
                    waits=all_ck, inc="cc")
                t_ccw = ("cc", p.cnt["cc"], "gpsimd", True)

                cv_fm = sb1("cv_fm", (128, NP * NCK))
                t_zk = p.emit("vector", lambda e: e.memset(ck_sb[:], 0.0), inc="dve")
                for hb in range(2):
                    ckv = ck_sb[hb * 64:(hb + 1) * 64].rearrange(
                        "p (g n) -> p g n", g=NP)[:, :, hb * NCK:hb * NCK + NCK]
                    for r in range(2):
                        p.dma("gpsimd", "qld",
                              ckv[:, :, 4 + r * NBLK: 4 + (r + 1) * NBLK],
                              ck_all[r, 0, :, hb * 64:(hb + 1) * 64, :].rearrange(
                                  "g p n -> p g n"),
                              waits=[t_ccw, t_zk])
                    p.dma("gpsimd", "qld", ckv[:, :, 0:4],
                          mem_l[0, :, hb * 64:(hb + 1) * 64, :].rearrange("g p n -> p g n"))
                for r in range(2):
                    p.dma("gpsimd", "qld",
                          cv_fm[:].rearrange("p (g n) -> p g n", g=NP)[
                              :, :, 4 + r * NBLK: 4 + (r + 1) * NBLK],
                          ck_all[r, 1].rearrange("g p n -> p g n"), waits=[t_ccw])
                p.dma("gpsimd", "qld",
                      cv_fm[:].rearrange("p (g n) -> p g n", g=NP)[:, :, 0:4],
                      mem_l[1].rearrange("g p n -> p g n"))
                t_ckl = ("qld", p.cnt["qld"], "gpsimd", True)
                t_z = p.emit("vector", lambda e: e.memset(cvt_sb[:], 0.0), inc="dve")
                for h in range(H):
                    ts, twt = trot.acquire()
                    b0 = (h % 2) * 64
                    t_tp = p.emit("tensor", lambda e, h=h, ts=ts, b0=b0: e.transpose(
                        pst[ts][0:NCK, 0:64],
                        cv_fm[b0:b0 + 64,
                              (h // 2) * NCK:(h // 2 + 1) * NCK],
                        ident[b0:b0 + 64, b0:b0 + 64]), waits=twt + [t_ckl], inc="pe")
                    t_ev = p.emit("vector", lambda e, h=h, ts=ts: e.tensor_copy(
                        cvt_sb[0:NCK, h * 64:(h + 1) * 64], pst[ts][0:NCK, 0:64]),
                        waits=[t_tp, t_z], inc="dve")
                    trot.release(ts, t_ev)

                if debug:
                    p.dma("gpsimd", "qdb", cko_dbg.rearrange("a b c d -> (a b c) d"),
                          ck_own.rearrange("a b c d -> (a b c) d"))
                    p.dma("gpsimd", "qdb", cw_dbg[:, :], cw_tm[:], waits=[("act", p.cnt["act"], "scalar", False)])
                    p.dma("gpsimd", "qdb", ckt_dbg[0, :, 0:NP * 2 * NCK], ck_sb[:])
                    p.dma("gpsimd", "qdb", ckt_dbg[1, :, 0:H * 64], cvt_sb[:])

                p1_join = [(s, p.cnt[s], "none", True) for s in p.cnt]

                with nc.Block() as blk:
                    p.replay(blk, sems)

            t_spq = [("qw0", p.cnt.get("qw0", 0), "g", True),
                     ("qw1", p.cnt.get("qw1", 0), "g", True)]
            t_spk = [("kw0", p.cnt.get("kw0", 0), "g", True),
                     ("kw1", p.cnt.get("kw1", 0), "g", True)]
            ph3 = contextlib.ExitStack()
            sb3 = lambda name, shape, dt=BF16: ph3.enter_context(nc.sbuf_tensor(name, list(shape), dt))
            ps3 = lambda name, shape, dt=F32: ph3.enter_context(nc.psum_tensor(name, list(shape), dt))
            with ph3:
                q_sb = sb3("q_sb", (128, NF * 512))
                vtm = sb3("vtm", (128, 5 * DIM))
                att = sb3("att", (128, NF * 512))
                wpad = [sb3(f"wpad{i}", (128, 2 * 640)) for i in range(2)]
                wout_sb = [sb3(f"wos{i}", (128, DIM)) for i in range(NF)]
                P_sb = [sb3(f"Pp{i}", (128, 648)) for i in range(3)]
                Pm_sb = [sb3(f"Pm{i}", (128, 648)) for i in range(3)]
                PT_sb = [sb3(f"PTp{i}", (128, 768)) for i in range(3)]
                y3_sb = [sb3(f"y3{i}", (128, 512), F32) for i in range(2)]
                xres3 = sb3("xres3", (128, 4 * 1024), F32)
                x1st3 = sb3("x1st3", (128, 4 * 1024), F32)
                lcol_ = [sb3(f"lcol{i}", (128, 16), F32) for i in range(3)]
                gcol_ = [sb3(f"gcol{i}", (128, 16), F32) for i in range(3)]
                spc3 = sb3("spc3", (128, 2), F32)

                ps_S = [ps3(f"ps_S{i}", (128, 512)) for i in range(2)]
                ps_C = [ps3(f"ps_C{i}", (128, 136)) for i in range(2)]
                ps_Ta = ps3("ps_Ta", (128, 512))
                ps_Tb = ps3("ps_Tb", (128, 256))
                ps_O = [ps3("ps_O0", (128, 128))]
                ps_W = ps3("ps_W", (128, 512))

                p.set_barrier(p.engine_totals())
                for kc in range(NF):
                    p.dma("gpsimd", "qsl", wout_sb[kc][:],
                          wout[kc * 128:(kc + 1) * 128, :])
                t_wout = ("qsl", p.cnt["qsl"], "g", True)
                for dst, src_ in ((gate_bf, gateT), (up_bf, upT), (down_bf, downT)):
                    for hf in range(4):
                        r0, r1 = hf * (dst.shape[0] // 4), (hf + 1) * (dst.shape[0] // 4)
                        p.dma("gpsimd", "qw", dst[r0:r1, :], src_[r0:r1, :])
                t_wbf = ("qw", p.cnt["qw"], "g", True)
                t_z1 = p.emit("vector", lambda e: e.memset(wpad[0][:], 0.0), inc="dve")
                t_z2 = p.emit("vector", lambda e: e.memset(wpad[1][:], 0.0), inc="dve")
                t_z3 = None
                for i3 in range(3):
                    t_z3 = p.emit("vector", lambda e, i3=i3: e.memset(
                        PT_sb[i3][64:128, 512:768], 0.0), inc="dve")

                rotS = Rot(2)
                rotP = Rot(3)
                wpr = Rot(2)
                yrot = Rot(2)
                ytr = Rot(2)
                oro = Rot(1)
                ptr = Rot(1)
                prev_q_user = None
                prev_v_user = None
                prev_att_user = None
                prev_wevac = None
                prev_x1_user = [None, None]
                prev_store = [[], []]
                SP3 = lambda: p.emit("vector", lambda e: e.tensor_copy(spc3[:, 0:1], lcol[:, 0:1]))
                for c in range(4):
                    t_q = p.dma("sync", "sB0",
                                q_sb[:].rearrange("p (f t) -> p f t", f=NF),
                                q_sp[:, :, c * 512:(c + 1) * 512].rearrange("f p t -> p f t"),
                                waits=t_spq + [prev_q_user])
                    t_q = ("sB0", p.cnt["sB0"], "g", True)
                    t_v = p.dma("sync", "sD0",
                                vtm[:].rearrange("p (w f) -> p w f", w=5),
                                v_sp[c * 4:c * 4 + 5, :, :].rearrange("w p f -> p w f"),
                                waits=t_spk + [prev_v_user])
                    t_v = ("sD0", p.cnt["sD0"], "g", True)
                    last_pe3 = None
                    pendq = []

                    def emit_tail(it):
                        nonlocal last_pe3, prev_att_user
                        pP, g, qc, tc, t_exp = it
                        lcol = lcol_[pP]
                        gcol = gcol_[pP]
                        cwb = tc * 96 + 6 * g
                        t_r1 = p.emit("vector", lambda e, lcol=lcol: e.reciprocal(
                            lcol[:, 4:6], lcol[:, 0:2]),
                            waits=[t_exp])
                        p.emit("vector", lambda e, lcol=lcol: e.reciprocal(
                            lcol[:, 6:8], lcol[:, 2:4]))
                        p.emit("vector", lambda e, cwb=cwb, gcol=gcol: e.tensor_add(
                            gcol[:, 4:6], cw_tm[:, cwb:cwb + 4:3],
                            cw_tm[:, cwb + 1:cwb + 5:3]))
                        SP3(); SP3()
                        p.emit("vector", lambda e, gcol=gcol, lcol=lcol: e.tensor_mul(
                            gcol[:, 6:8], gcol[:, 4:6], lcol[:, 4:6]))
                        t_gt = p.emit("vector", lambda e, cwb=cwb, gcol=gcol, lcol=lcol: e.tensor_mul(
                            gcol[:, 8:10], cw_tm[:, cwb + 2:cwb + 6:3],
                            lcol[:, 6:8]), inc="dve")
                        t_poolS = None
                        for hb in range(2):
                            t_poolS = p.emit("vector", lambda e, pP=pP, hb=hb, gcol=gcol: e.tensor_scalar_mul(
                                Pm_sb[pP][:, hb * 256:(hb + 1) * 256],
                                P_sb[pP][:, hb * 256:(hb + 1) * 256],
                                gcol[:, 6 + hb:7 + hb]),
                                waits=[t_gt] if hb == 0 else [],
                                inc="dve" if hb == 1 else None)
                        t_poolC = None
                        for hb in range(2):
                            t_poolC = p.emit("vector", lambda e, pP=pP, hb=hb, gcol=gcol: e.tensor_scalar_mul(
                                Pm_sb[pP][:, 512 + hb * 68:512 + (hb + 1) * 68],
                                P_sb[pP][:, 512 + hb * 68:512 + (hb + 1) * 68],
                                gcol[:, 8 + hb:9 + hb]),
                                inc="dve" if hb == 1 else None)
                        pts, ptw = ptr.acquire()
                        for k6 in range(4):
                            p.emit("tensor", lambda e, k6=k6, pP=pP: e.matmul(
                                ps_Ta[:, k6 * 128:(k6 + 1) * 128],
                                Pm_sb[pP][:, k6 * 128:(k6 + 1) * 128], ident[:, :],
                                start=(k6 == 0), stop=(k6 == 3)),
                                waits=[t_poolS, t_poolC] + (ptw if k6 == 0 else []))
                        tt = None
                        for hb in range(2):
                            tt = p.emit("tensor", lambda e, hb=hb, pP=pP: e.matmul(
                                ps_Tb[0:NCK, hb * 128:hb * 128 + 128],
                                Pm_sb[pP][:, 512 + hb * 68:512 + (hb + 1) * 68],
                                ident[:, :], start=(hb == 0), stop=(hb == 1)),
                                inc="pe" if hb == 1 else None)
                        p.emit("vector", lambda e, pP=pP: e.tensor_copy(
                            PT_sb[pP][:, 0:512], ps_Ta[:, 0:512]),
                            waits=[tt, t_z3])
                        t_ptev = p.emit("vector", lambda e, pP=pP: e.tensor_copy(
                            PT_sb[pP][0:NCK, 512:768], ps_Tb[0:NCK, 0:256]),
                            inc="dve")
                        ptr.release(pts, t_ptev)
                        so, sow = oro.acquire()
                        lastO = None
                        for hb in range(2):
                            for st3 in range(3):
                                if st3 < 2:
                                    lhs = vtm[:, (qc + st3) * DIM + (2 * g + hb) * 64:
                                              (qc + st3) * DIM + (2 * g + hb) * 64 + 64]
                                    rhs = PT_sb[pP][:, hb * 256 + st3 * 128:
                                                    hb * 256 + (st3 + 1) * 128]
                                else:
                                    lhs = cvt_sb[:, (2 * g + hb) * 64:(2 * g + hb + 1) * 64]
                                    rhs = PT_sb[pP][:, 512 + hb * 128:512 + (hb + 1) * 128]
                                lastO = p.emit("tensor", lambda e, lhs=lhs, rhs=rhs, hb=hb,
                                               st3=st3, so=so: e.matmul(
                                    ps_O[so][hb * 64:(hb + 1) * 64, :], lhs, rhs,
                                    start=(st3 == 0), stop=(st3 == 2),
                                    tile_position=(0, hb * 64)),
                                    waits=([t_ptev, t_v, t_ckl] + sow
                                           if (hb == 0 and st3 == 0) else []),
                                    inc="pe" if (hb == 1 and st3 == 2) else None)
                        t_oev = p.emit("scalar", lambda e, g=g, qc=qc, so=so: e.copy(
                            att[:, g * 512 + qc * 128: g * 512 + (qc + 1) * 128],
                            ps_O[so][:, :]), waits=[lastO], inc="act")
                        oro.release(so, t_oev)
                        prev_att_user = t_oev
                        last_pe3 = lastO
                        rotP.release(pP, t_ptev)
                        rotP.release(pP, lastO)
                        rotP.release(pP, t_poolS)
                        rotP.release(pP, t_oev)

                    for g in range(NP):
                        ws, wwt = wpr.acquire()
                        p.dma("sync", f"sC{ws}", wpad[ws][0:64, 0:640],
                              k_sp[g, 0:64, c * 512:c * 512 + 640],
                              waits=t_spk + wwt + [t_z1 if ws == 0 else t_z2])
                        p.dma("sync", f"sC{ws}", wpad[ws][64:128, 640:1280],
                              k_sp[g, 64:128, c * 512:c * 512 + 640])
                        t_wp = (f"sC{ws}", p.cnt[f"sC{ws}"], "g", True)
                        for qc in range(4):
                            tc = c * 4 + qc
                            sS, sw = rotS.acquire()
                            pP, pw = rotP.acquire()
                            lcol = lcol_[pP]
                            gcol = gcol_[pP]
                            lq = q_sb[:, g * 512 + qc * 128: g * 512 + (qc + 1) * 128]
                            rhsS = wpad[ws].rearrange("p (g2 w) -> p g2 w", g2=2)[
                                :, :, qc * 128:qc * 128 + 256]
                            p.emit("tensor", lambda e, sS=sS, lq=lq, rhsS=rhsS: e.matmul(
                                ps_S[sS][:, :], lq, rhsS, start=True, stop=False),
                                waits=[t_q, t_wp] + sw)
                            p.emit("tensor", lambda e, sS=sS, lq=lq, g=g: e.matmul(
                                ps_C[sS][:, :], lq,
                                ck_sb[:, g * 2 * NCK:(g + 1) * 2 * NCK],
                                start=True, stop=False), waits=[t_ckl])
                            slot = 0 if tc == 0 else 1
                            p.emit("tensor", lambda e, sS=sS, slot=slot: e.matmul(
                                ps_S[sS][:, :], ident[:, :],
                                lmask[:, slot * 512:(slot + 1) * 512],
                                start=False, stop=True), waits=[t_qc, t_id])
                            t_mm = p.emit("tensor", lambda e, sS=sS, tc=tc: e.matmul(
                                ps_C[sS][:, :], ident[:, :],
                                cmask[:, tc * 136:(tc + 1) * 136],
                                start=False, stop=True), inc="pe")
                            t_nm = p.emit("vector", lambda e, sS=sS, gcol=gcol: e.reduce_max(
                                gcol[:, 0:2],
                                ps_S[sS][:].rearrange("p (h n) -> p h n", h=2),
                                AX.X, negate=True),
                                waits=[t_mm] + sw + pw, inc="dve")
                            t_nc2 = p.emit("vector", lambda e, sS=sS, gcol=gcol: e.reduce_max(
                                gcol[:, 2:4],
                                ps_C[sS][:].rearrange("p (h n) -> p h n", h=2),
                                AX.X, negate=True), inc="dve")
                            ee = []
                            for hb in range(2):
                                ee.append(p.emit("scalar", lambda e, sS=sS, pP=pP, hb=hb, gcol=gcol, lcol=lcol: e.activation(
                                    P_sb[pP][:, hb * 256:(hb + 1) * 256],
                                    ps_S[sS][:, hb * 256:(hb + 1) * 256], AF.Exp,
                                    bias=gcol[:, hb:hb + 1],
                                    accum_out=lcol[:, hb:hb + 1]),
                                    waits=[t_nm, t_nc2], inc="act"))
                            for hb in range(2):
                                ee.append(p.emit("scalar", lambda e, sS=sS, pP=pP, hb=hb, gcol=gcol, lcol=lcol: e.activation(
                                    P_sb[pP][:, 512 + hb * 68:512 + (hb + 1) * 68],
                                    ps_C[sS][:, hb * 68:(hb + 1) * 68], AF.Exp,
                                    bias=gcol[:, 2 + hb:3 + hb],
                                    accum_out=lcol[:, 2 + hb:3 + hb]), inc="act"))
                            t_exp = ee[-1]
                            rotS.release(sS, ee[1])
                            rotS.release(sS, t_nm)
                            rotS.release(sS, t_nc2)
                            rotS.release(sS, ee[3])
                            pendq.append((pP, g, qc, tc, t_exp))
                            if len(pendq) > 2:
                                emit_tail(pendq.pop(0))
                        wpr.release(ws, last_pe3)
                    while pendq:
                        emit_tail(pendq.pop(0))
                    prev_q_user = last_pe3
                    prev_v_user = last_pe3
                    for dh2 in range(2):
                        t_xr = None
                        for tc4 in range(4):
                            t_xr = p.dma("gpsimd", "sA0",
                                         xres3[:, tc4 * 1024:(tc4 + 1) * 1024],
                                         x_in[E + c * 512 + tc4 * 128:
                                              E + c * 512 + (tc4 + 1) * 128,
                                              dh2 * 1024:(dh2 + 1) * 1024],
                                         waits=[prev_x1_user[0]] if tc4 == 0 else [])
                        t_xr = ("sA0", p.cnt["sA0"], "g", True)
                        for mh in range(8):
                            m = dh2 * 8 + mh
                            lastW = None
                            for kc in range(NF):
                                lastW = p.emit("tensor", lambda e, kc=kc, m=m: e.matmul(
                                    ps_W[:, :], wout_sb[kc][:, m * 128:(m + 1) * 128],
                                    att[:, kc * 512:(kc + 1) * 512],
                                    start=(kc == 0), stop=(kc == NF - 1)),
                                    waits=([t_wout, prev_att_user, prev_wevac]
                                           if kc == 0 else []),
                                    inc="pe" if kc == NF - 1 else None)
                            ys, ysw = yrot.acquire()
                            t_ye = p.emit("vector", lambda e, ys=ys: e.tensor_copy(
                                y3_sb[ys][:, :], ps_W[:, :]), waits=[lastW] + ysw,
                                inc="dve")
                            prev_wevac = t_ye
                            for tc4 in range(4):
                                ts2, tsw = oro.acquire()
                                t_yt = p.emit("tensor", lambda e, ys=ys, ts2=ts2, tc4=tc4: e.transpose(
                                    ps_O[ts2][:, :],
                                    y3_sb[ys][:, tc4 * 128:(tc4 + 1) * 128],
                                    ident32[:, :]), waits=[t_ye] + tsw, inc="pe")
                                t_ya = p.emit("vector", lambda e, ts2=ts2, tc4=tc4, mh=mh: e.tensor_add(
                                    x1st3[:, tc4 * 1024 + mh * 128:
                                          tc4 * 1024 + (mh + 1) * 128],
                                    ps_O[ts2][:, :],
                                    xres3[:, tc4 * 1024 + mh * 128:
                                          tc4 * 1024 + (mh + 1) * 128]),
                                    waits=[t_yt, t_xr] + prev_store[0], inc="dve")
                                prev_store[0] = []
                                oro.release(ts2, t_ya)
                                yrot.release(ys, t_ya)
                                prev_x1_user[0] = t_ya
                        stt3 = []
                        for tc4 in range(4):
                            p.dma("gpsimd", "out0",
                                  x1_sp[c * 512 + tc4 * 128: c * 512 + (tc4 + 1) * 128,
                                        dh2 * 1024:(dh2 + 1) * 1024],
                                  x1st3[:, tc4 * 1024:(tc4 + 1) * 1024],
                                  waits=[prev_x1_user[0]])
                        prev_store[0] = [("out0", p.cnt["out0"], "g", True)]

                t_x1w = ("out0", p.cnt["out0"], "g", True)
                with nc.Block() as blk:
                    p.replay(blk, sems)

            ph13.close()

            NG = 2
            GT = 1024
            NTC4 = GT // 128
            ph4o = contextlib.ExitStack()
            sb4o = lambda name, shape, dt=BF16: ph4o.enter_context(nc.sbuf_tensor(name, list(shape), dt))
            with ph4o:
                hdn = sb4o("hdn", (128, NI * GT))
                for grp4 in range(NG):
                    tok0 = grp4 * GT
                    ph4 = contextlib.ExitStack()
                    sb4 = lambda name, shape, dt=BF16: ph4.enter_context(nc.sbuf_tensor(name, list(shape), dt))
                    ps4 = lambda name, shape, dt=F32: ph4.enter_context(nc.psum_tensor(name, list(shape), dt))
                    with ph4:
                        p.set_barrier(p.engine_totals() +
                                      [("out0", p.cnt.get("out0", 0), "g", True)])
                        x4 = [sb4(f"x4_{grp4}_{i}", (128, DIM), F32) for i in range(2)]
                        sq4 = sb4(f"sq4_{grp4}", (128, DIM), F32)
                        xf_tm = [sb4(f"xftm{grp4}_{i}", (128, DIM)) for i in range(2)]
                        xf = sb4(f"xf_{grp4}", (128, NF * GT))
                        gsl = [sb4(f"gsl{grp4}_{i}", (128, NF * 384)) for i in range(2)]
                        usl = [sb4(f"usl{grp4}_{i}", (128, NF * 384)) for i in range(2)]
                        sil = sb4(f"sil_{grp4}", (128, 1024), F32)
                        s4a = sb4(f"s4a_{grp4}", (128, 16), F32)
                        s4b = sb4(f"s4b_{grp4}", (128, 16), F32)
                        s4c = sb4(f"s4c_{grp4}", (128, 16), F32)
                        s4d = sb4(f"s4d_{grp4}", (128, 16), F32)
                        s4e = sb4(f"s4e_{grp4}", (128, 16), F32)
                        ps_g = [ps4(f"ps_g{grp4}_{i}", (128, 512)) for i in range(2)]
                        ps_u = [ps4(f"ps_u{grp4}_{i}", (128, 512)) for i in range(2)]
                        ps_x = [ps4(f"ps_x{grp4}_{i}", (128, 128), BF16) for i in range(2)]

                        x4rot = Rot(2)
                        for tc in range(NTC4):
                            s, wt = x4rot.acquire()
                            t_x = p.dma("sync", f"qx{s}", x4[s][:],
                                        x1_sp[tok0 + tc * 128: tok0 + (tc + 1) * 128, :],
                                        waits=wt + [t_x1w])
                            v = "vector"
                            p.emit(v, lambda e, s=s: e.tensor_mul(sq4[:], x4[s][:], x4[s][:]),
                                   waits=[t_x])
                            p.emit(v, lambda e, s=s, tc=tc: e.reduce_sum(
                                s4a[:, tc:tc + 1], x4[s][:], AX.X))
                            t_r = p.emit(v, lambda e, tc=tc: e.reduce_sum(
                                s4b[:, tc:tc + 1], sq4[:], AX.X), inc="dve")
                            x4rot.release(s, t_r)
                        v = "vector"
                        SP4 = lambda: p.emit(v, lambda e: e.tensor_copy(s4e[:, 15:16],
                                                                        s4a[:, 0:1]))
                        SP4(); SP4(); SP4()
                        p.emit(v, lambda e: e.tensor_scalar_mul(s4c[:, 0:8], s4a[:, 0:8], 1.0 / DIM))
                        p.emit(v, lambda e: e.tensor_scalar_mul(s4d[:, 0:8], s4b[:, 0:8], 1.0 / DIM))
                        SP4(); SP4()
                        p.emit(v, lambda e: e.tensor_mul(s4e[:, 0:8], s4c[:, 0:8], s4c[:, 0:8]))
                        SP4(); SP4()
                        t_var4 = p.emit(v, lambda e: e.tensor_sub(s4d[:, 0:8], s4d[:, 0:8], s4e[:, 0:8]),
                                        inc="dve")
                        t_sq4 = p.emit("scalar", lambda e: e.activation(
                            s4e[:, 0:8], s4d[:, 0:8], AF.Sqrt, bias=eps4_c[:, :]),
                            waits=[t_var4, t_ep4], inc="act")
                        p.emit(v, lambda e: e.reciprocal(s4d[:, 8:16], s4e[:, 0:8]), waits=[t_sq4])
                        SP4(); SP4(); SP4()
                        p.emit(v, lambda e: e.tensor_mul(s4c[:, 8:16], s4c[:, 0:8], s4d[:, 8:16]))
                        SP4(); SP4()
                        t_st4 = p.emit(v, lambda e: e.tensor_scalar_mul(
                            s4c[:, 8:16], s4c[:, 8:16], -1.0), inc="dve")
                        SP4(); SP4(); SP4()
                        xtra = Rot(2)
                        xf_done = None
                        xh_rd = [None, None]
                        for tc in range(NTC4):
                            s, wt = x4rot.acquire()
                            t_x = p.dma("sync", f"qx{s}", x4[s][:],
                                        x1_sp[tok0 + tc * 128: tok0 + (tc + 1) * 128, :],
                                        waits=wt)
                            hs = tc % 2
                            t_xh = p.emit(v, lambda e, s=s, hs=hs, tc=tc: e.tensor_scalar(
                                xf_tm[hs][:], x4[s][:], s4d[:, 8 + tc:9 + tc],
                                s4c[:, 8 + tc:9 + tc], ALU.mult, ALU.add),
                                waits=[t_x, t_st4, xh_rd[hs]],
                                inc="dve")
                            x4rot.release(s, t_xh)
                            for f in range(NF):
                                ts, twt = xtra.acquire()
                                t_tp = p.emit("tensor", lambda e, f=f, hs=hs, ts=ts: e.transpose(
                                    ps_x[ts][:, :], xf_tm[hs][:, f * 128:(f + 1) * 128],
                                    ident[:, :]), waits=twt + [t_xh, t_id], inc="pe")
                                if f == NF - 1:
                                    xh_rd[hs] = t_tp
                                t_ev = p.emit(v, lambda e, f=f, tc=tc, ts=ts: e.tensor_copy(
                                    xf[:, f * GT + tc * 128: f * GT + (tc + 1) * 128],
                                    ps_x[ts][:, :]), waits=[t_tp], inc="dve")
                                xtra.release(ts, t_ev)
                                if f == NF - 1:
                                    xf_done = t_ev
                        NMG = 15
                        grot = Rot(2)
                        guguard = [[], []]
                        hdn_done = None
                        for mg in range(NMG):
                            mi0 = mg * 3
                            nmi = 3 if mg < 14 else 2
                            gs4, gw4 = grot.acquire()
                            t_g = p.dma("sync", f"kc{gs4}",
                                        gsl[gs4][:].rearrange("p (k m) -> p k m", k=NF)[
                                            :, :, 0:nmi * 128],
                                        gate_bf.rearrange("(k p) i -> p k i", p=128)[
                                            :, :, mi0 * 128:(mi0 + nmi) * 128],
                                        waits=gw4 + [t_wbf])
                            t_u = p.dma("sync", f"kc{gs4}",
                                        usl[gs4][:].rearrange("p (k m) -> p k m", k=NF)[
                                            :, :, 0:nmi * 128],
                                        up_bf.rearrange("(k p) i -> p k i", p=128)[
                                            :, :, mi0 * 128:(mi0 + nmi) * 128])
                            t_gu = (f"kc{gs4}", p.cnt[f"kc{gs4}"], "g", True)
                            lastgu = None
                            for mi in range(nmi):
                                m = mi0 + mi
                                for n2 in range(2):
                                    pgs = (m * 2 + n2) % 2
                                    gw9 = guguard[pgs]
                                    guguard[pgs] = []
                                    for which, slab, pp4 in ((0, gsl, ps_g), (1, usl, ps_u)):
                                        for kc in range(NF):
                                            lastgu = p.emit("tensor", lambda e, kc=kc, mi=mi, n2=n2, which=which, gs4=gs4, pgs=pgs, slab=slab, pp4=pp4: e.matmul(
                                                pp4[pgs][:, :],
                                                slab[gs4][:, kc * 384 + mi * 128:
                                                          kc * 384 + (mi + 1) * 128],
                                                xf[:, kc * GT + n2 * 512:
                                                   kc * GT + (n2 + 1) * 512],
                                                start=(kc == 0), stop=(kc == NF - 1)),
                                                waits=([t_gu, xf_done] + gw9
                                                       if kc == 0 else []),
                                                inc="pe" if kc == NF - 1 else None)
                                    t_si = p.emit("scalar", lambda e, pgs=pgs, m=m: e.activation(
                                        sil[:, pgs * 512:(pgs + 1) * 512], ps_g[pgs][:, :],
                                        AF.Silu,
                                        scale=scales_g[:, m:m + 1]), waits=[lastgu],
                                        inc="act")
                                    t_hd = p.emit("vector", lambda e, pgs=pgs, m=m, n2=n2: e.scalar_tensor_tensor(
                                        hdn[:, m * GT + n2 * 512: m * GT + (n2 + 1) * 512],
                                        ps_u[pgs][:, :], scales_u[:, m:m + 1],
                                        sil[:, pgs * 512:(pgs + 1) * 512],
                                        ALU.mult, ALU.mult), waits=[t_si], inc="dve")
                                    guguard[pgs] = [t_si, t_hd]
                                    hdn_done = t_hd
                            grot.release(gs4, lastgu)
                        t_p4a = [("pe", p.cnt["pe"], "n", True),
                                 ("dve", p.cnt["dve"], "n", True),
                                 ("act", p.cnt["act"], "n", True)]
                        with nc.Block() as blk:
                            p.replay(blk, sems)
                    ph4b = contextlib.ExitStack()
                    sb4b = lambda name, shape, dt=BF16: ph4b.enter_context(nc.sbuf_tensor(name, list(shape), dt))
                    ps4b = lambda name, shape, dt=F32: ph4b.enter_context(nc.psum_tensor(name, list(shape), dt))
                    with ph4b:
                        p.set_barrier(p.engine_totals() +
                                      [("out0", p.cnt.get("out0", 0), "g", True)])
                        dsl = [sb4b(f"dsl{grp4}_{i}", (128, NI * 128)) for i in range(2)]
                        y4 = [sb4b(f"y4_{grp4}_{i}", (128, 1024), F32) for i in range(2)]
                        xr4 = sb4b(f"xr4_{grp4}", (128, NTC4 * 512), F32)
                        o4 = sb4b(f"o4_{grp4}", (128, NTC4 * 512), F32)
                        ps_d = [ps4b(f"ps_d{grp4}_{i}", (128, 512)) for i in range(2)]
                        ps_y = [ps4b(f"ps_y{grp4}_{i}", (128, 128)) for i in range(2)]
                        drot = Rot(2)
                        y4r = Rot(2)
                        y4t = Rot(2)
                        dguard = [[], []]
                        pend4 = None
                        prev_o4 = [t_p4a[1], t_x1w]
                        prev_ost = []

                        def emit_tail4(it):
                            nonlocal prev_o4, prev_ost
                            ys4, m, mh4, t_ye4, _txr = it
                            for tc8 in range(8):
                                ts4, tsw4 = y4t.acquire()
                                t_yt4 = p.emit("tensor", lambda e, ys4=ys4, ts4=ts4, tc8=tc8: e.transpose(
                                    ps_y[ts4][:, :],
                                    y4[ys4][:, (tc8 % 4) * 128:(tc8 % 4 + 1) * 128]
                                    if False else
                                    y4[ys4][:, tc8 * 128:(tc8 + 1) * 128],
                                    ident32[:, :]), waits=[t_ye4] + tsw4, inc="pe")
                                t_ya4 = p.emit("vector", lambda e, ts4=ts4, tc8=tc8, mh4=mh4: e.tensor_add(
                                    o4[:, tc8 * 512 + mh4 * 128:
                                       tc8 * 512 + (mh4 + 1) * 128],
                                    ps_y[ts4][:, :],
                                    xr4[:, tc8 * 512 + mh4 * 128:
                                        tc8 * 512 + (mh4 + 1) * 128]),
                                    waits=[t_yt4, _txr] + prev_ost, inc="dve")
                                prev_ost = []
                                y4t.release(ts4, t_ya4)
                                y4r.release(ys4, t_ya4)
                                prev_o4 = [t_ya4]
                        for mq in range(4):
                            t_xr4 = None
                            for tc4 in range(NTC4):
                                t_xr4 = p.dma("gpsimd", "sA1",
                                              xr4[:, tc4 * 512:(tc4 + 1) * 512],
                                              x1_sp[tok0 + tc4 * 128: tok0 + (tc4 + 1) * 128,
                                                    mq * 512:(mq + 1) * 512],
                                              waits=(prev_o4 if tc4 == 0 else []))
                            t_xr4 = ("sA1", p.cnt["sA1"], "g", True)
                            prev_o4 = []
                            for mh4 in range(4):
                                m = mq * 4 + mh4
                                ds4, dw4 = drot.acquire()
                                p.dma("sync", f"kx{ds4}",
                                      dsl[ds4][:].rearrange("p (k m) -> p k m", k=NI),
                                      down_bf.rearrange("(k p) d -> p k d", p=128)[
                                          :, :, m * 128:(m + 1) * 128],
                                      waits=dw4 + [t_wbf])
                                t_ds = (f"kx{ds4}", p.cnt[f"kx{ds4}"], "g", True)
                                ys4, ysw4 = y4r.acquire()
                                t_last_ev = None
                                for n2 in range(2):
                                    pds = n2
                                    dw9 = dguard[pds]
                                    dguard[pds] = []
                                    lastd = None
                                    for kc in range(NI):
                                        lastd = p.emit("tensor", lambda e, kc=kc, ds4=ds4, pds=pds, n2=n2: e.matmul(
                                            ps_d[pds][:, :],
                                            dsl[ds4][:, kc * 128:(kc + 1) * 128],
                                            hdn[:, kc * GT + n2 * 512:
                                                kc * GT + (n2 + 1) * 512],
                                            start=(kc == 0), stop=(kc == NI - 1)),
                                            waits=([t_ds, hdn_done] + dw9 + (ysw4 if n2 == 0 else [])
                                                   if kc == 0 else []),
                                            inc="pe" if kc == NI - 1 else None)
                                    t_ye4 = p.emit("vector", lambda e, ys4=ys4, pds=pds, m=m, n2=n2: e.tensor_scalar_mul(
                                        y4[ys4][:, n2 * 512:(n2 + 1) * 512],
                                        ps_d[pds][:, :],
                                        scales_d[:, m:m + 1]), waits=[lastd],
                                        inc="dve")
                                    dguard[pds] = [t_ye4]
                                    t_last_ev = t_ye4
                                if pend4 is not None:
                                    emit_tail4(pend4)
                                pend4 = (ys4, m, mh4, t_last_ev, t_xr4)
                                drot.release(ds4, lastd)
                            if pend4 is not None:
                                emit_tail4(pend4)
                                pend4 = None
                            for tc4 in range(NTC4):
                                p.dma("gpsimd", "out1",
                                      out_h[tok0 + tc4 * 128: tok0 + (tc4 + 1) * 128,
                                            mq * 512:(mq + 1) * 512],
                                      o4[:, tc4 * 512:(tc4 + 1) * 512],
                                      waits=[prev_o4[0]])
                            prev_ost = [("out1", p.cnt["out1"], "g", True)]
                        with nc.Block() as blk:
                            p.replay(blk, sems)

            def final_wait(e):
                for s, v in p.cnt.items():
                    e.wait_ge(sems[s], v)
            with nc.Block() as blk:
                blk.gpsimd(final_wait)

    return nc


def _prep_shared(inp):
    f32 = np.float32
    d = {}
    d["wqkv"] = np.ascontiguousarray(inp["L0_attn_w_qkv"], f32)
    d["rmsw_c"] = np.ascontiguousarray(inp["L0_attn_rms_w"].reshape(NF, 128).T, f32)
    d["wcomb"] = np.ascontiguousarray(inp["L0_attn_w_comb"], f32)
    d["bcomb"] = np.ascontiguousarray(inp["L0_attn_b_comb"].reshape(1, 96), f32)
    kc, vc = inp["L0_attn_kc_w"], inp["L0_attn_vc_w"]

    def pair_itow(w):
        ev = w[0::2].transpose(0, 2, 3, 1)
        od = w[1::2].transpose(0, 2, 3, 1)
        return np.concatenate([ev, od], axis=1)
    d["kcw_l"] = np.ascontiguousarray(
        np.stack([pair_itow(kc), pair_itow(vc)], axis=1), f32)
    kp, vp = inp["L0_attn_k_pos"], inp["L0_attn_v_pos"]

    def pair_pos(w):
        ev = w[0::2].transpose(0, 2, 1)
        od = w[1::2].transpose(0, 2, 1)
        return np.concatenate([ev, od], axis=1)
    d["kpos_l"] = np.ascontiguousarray(
        np.stack([pair_pos(kp), pair_pos(vp)], axis=1), f32)
    kb, vb = inp["L0_attn_kc_b"], inp["L0_attn_vc_b"]

    def pair_b(w):
        return np.concatenate([w[0::2], w[1::2]], axis=1).T
    d["kcb_l"] = np.ascontiguousarray(
        np.concatenate([pair_b(kb), pair_b(vb)], axis=1), f32)
    mm = inp["L0_attn_mem_kv"]

    def pair_m(w):
        ev = w[0::2].transpose(0, 2, 1)
        od = w[1::2].transpose(0, 2, 1)
        return np.concatenate([ev, od], axis=1)
    d["mem_l"] = np.ascontiguousarray(np.stack([pair_m(mm[0]), pair_m(mm[1])]), f32)
    d["wout"] = np.ascontiguousarray(inp["L0_attn_w_out"], f32)
    d["gateT"] = np.ascontiguousarray(inp["L0_ffn_gate_w"].T, f32)
    d["upT"] = np.ascontiguousarray(inp["L0_ffn_up_w"].T, f32)
    d["downT"] = np.ascontiguousarray(inp["L0_ffn_down_w"].T, f32)
    d["gs_c"] = np.ascontiguousarray(inp["L0_ffn_gate_s"].reshape(NI, 128).T, f32)
    d["us_c"] = np.ascontiguousarray(inp["L0_ffn_up_s"].reshape(NI, 128).T, f32)
    d["ds_c"] = np.ascontiguousarray(inp["L0_ffn_down_s"].reshape(NF, 128).T, f32)
    d["ident_h"] = np.eye(128, dtype=BF)
    d["ident32_h"] = np.eye(128, dtype=np.float32)
    return d


def _masks(h):
    i = np.arange(128)[:, None]
    j = np.arange(NKK)[None, :]
    std = (i <= j) & (j <= i + 128)
    first = std & ((j >= 128) if h == 0 else True)
    lm = np.full((2, 128, NKK), NEG, np.float32)
    lm[0][first] = 0.0
    lm[1][std] = 0.0
    cm = np.full((16, 128, NCK), NEG, np.float32)
    for tc in range(16):
        ig = h * T + tc * 128 + np.arange(128)[:, None]
        g = np.arange(NCK)[None, :] - 4
        ok = (g >= 0) & (g >= ig // CBS)
        cm[tc][np.broadcast_to(ok, (128, NCK))] = 0.0
    lm2 = np.concatenate([lm, lm], axis=2)
    cm2 = np.concatenate([cm, cm], axis=2)
    return lm2.astype(BF), cm2.astype(BF)


def make_in_maps(inp):
    shared = _prep_shared(inp)
    x = np.asarray(inp["L0_attn_inp"], np.float32)
    maps = []
    for c in range(8):
        b, h = c // 2, c % 2
        xs = np.zeros((RN, DIM), np.float32)
        xs[E:] = x[b, h * T:(h + 1) * T]
        if h == 1:
            xs[:E] = x[b, T - E:T]
        lm, cm = _masks(h)
        m = dict(shared)
        m["x_in"] = xs
        m["lmask_h"] = lm
        m["cmask_h"] = cm
        maps.append(m)
    return maps


def kernel(**inp):
    nc = build_nc(debug=0)
    maps = make_in_maps(inp)
    res = run_bass_kernel_spmd(nc, maps, core_ids=list(range(8)))
    out = np.empty((4, 4096, DIM), np.float32)
    for c in range(8):
        b, h = c // 2, c % 2
        out[b, h * T:(h + 1) * T] = res.results[c]["out"]
    return out


# version 13
# speedup vs baseline: 1.0186x; 1.0186x over previous
import numpy as np
import ml_dtypes
import concourse.bass as bass
import concourse.mybir as mybir
from concourse.bass_utils import run_bass_kernel_spmd

F32 = mybir.dt.float32
BF16 = mybir.dt.bfloat16
AF = mybir.ActivationFunctionType
ALU = mybir.AluOpType
AX = mybir.AxisListType
BF = ml_dtypes.bfloat16

H, DH, DIM = 32, 64, 2048
T, E = 2048, 128
RN = T + E
NF = DIM // 128
NP = H // 2
SCALE = DH ** -0.5
NEG = -1e30
EPS_LN = 1e-5
EPS_RMS = float(np.finfo(np.float32).eps)
INTER = 5632
NI = INTER // 128
CBS = 64
NBLK = T // CBS
NCK = 68
NKK = 256


ENGINES = ("sync", "tensor", "vector", "scalar", "gpsimd")


class Prog:

    def __init__(self):
        self.ops = {e: [] for e in ENGINES}
        self.cnt = {}
        self.pending = {}

    def set_barrier(self, tokens):
        for e in ENGINES:
            self.pending[e] = list(tokens)

    def engine_totals(self):
        return [(s, v, "n", True) for s, v in self.cnt.items()
                if s in ("pe", "dve", "act", "pool")]

    def emit(self, eng, fn, waits=(), inc=None, amt=1):
        wl = []
        waits = list(waits) + self.pending.pop(eng, [])
        for w in waits:
            if w is None:
                continue
            sem, val, peng, asy = w
            if peng == eng and not asy:
                continue
            wl.append((sem, val))
        tok = None
        if inc is not None:
            self.cnt[inc] = self.cnt.get(inc, 0) + amt
            tok = (inc, self.cnt[inc], eng, eng in ("sync", "gpsimd") and amt % 16 == 0)
        self.ops[eng].append((fn, wl, inc, amt))
        return tok

    def dma(self, queue, sem, out, in_, waits=(), **kw):
        return self.emit(
            queue,
            lambda e: e.dma_start(out=out, in_=in_, **kw),
            waits=waits, inc=sem, amt=16)

    def replay(self, block, sems):
        for eng in ENGINES:
            section = self.ops[eng]
            if not section:
                continue

            def run(engine, section=section):
                seen = {}
                for fn, waits, inc, amt in section:
                    for s, v in waits:
                        if seen.get(s, -1) >= v:
                            continue
                        engine.wait_ge(sems[s], v)
                        seen[s] = v
                    ins = fn(engine)
                    if inc is not None:
                        ins.then_inc(sems[inc], amt)
            getattr(block, eng)(run)
        self.ops = {e: [] for e in ENGINES}


class Rot:

    def __init__(self, n):
        self.n = n
        self.tok = [[] for _ in range(n)]
        self.i = 0

    def acquire(self):
        s = self.i % self.n
        self.i += 1
        waits = self.tok[s]
        self.tok[s] = []
        return s, waits

    def release(self, s, tok):
        if tok is not None:
            self.tok[s].append(tok)


def build_nc(debug=0):
    nc = bass.Bass("TRN2", target_bir_lowering=False,
                   detect_race_conditions=False)
    p = Prog()

    def din(name, shape, dtype=F32):
        return nc.dram_tensor(name, list(shape), dtype, kind="ExternalInput").ap()

    x_in = din("x_in", (RN, DIM))
    wqkv = din("wqkv", (DIM, 3 * DIM))
    rmsw_c = din("rmsw_c", (128, NF))
    wcomb = din("wcomb", (DIM, 96))
    bcomb = din("bcomb", (1, 96))
    kcw_l = din("kcw_l", (NP, 2, 128, CBS, 64))
    kpos_l = din("kpos_l", (NP, 2, 128, CBS))
    kcb_l = din("kcb_l", (128, 2 * NP))
    mem_l = din("mem_l", (2, NP, 128, 4))
    wout = din("wout", (DIM, DIM))
    gateT = din("gateT", (DIM, INTER))
    upT = din("upT", (DIM, INTER))
    downT = din("downT", (INTER, DIM))
    gate_bf = nc.dram_tensor("gate_bf", [DIM, INTER], BF16, kind="Internal").ap()
    up_bf = nc.dram_tensor("up_bf", [DIM, INTER], BF16, kind="Internal").ap()
    down_bf = nc.dram_tensor("down_bf", [INTER, DIM], BF16, kind="Internal").ap()
    gs_c = din("gs_c", (128, NI))
    us_c = din("us_c", (128, NI))
    ds_c = din("ds_c", (128, NF))
    ident_h = din("ident_h", (128, 128), BF16)
    ident32_h = din("ident32_h", (128, 128), F32)
    lmask_h = din("lmask_h", (2, 128, 2 * NKK), BF16)
    cmask_h = din("cmask_h", (16, 128, 2 * NCK), BF16)

    out_h = nc.dram_tensor("out", [T, DIM], F32, kind="ExternalOutput").ap()

    kind_sp = "ExternalOutput" if debug else "Internal"
    q_sp = nc.dram_tensor("q_sp", [NF, 128, T], BF16, kind=kind_sp).ap()
    k_sp = nc.dram_tensor("k_sp", [NF, 128, RN], BF16, kind=kind_sp).ap()
    v_sp = nc.dram_tensor("v_sp", [NF + 1, 128, DIM], BF16, kind=kind_sp).ap()
    ck_own = nc.dram_tensor("ck_own", [2, NP, 128, NBLK], F32, kind="Internal").ap()
    ck_all = nc.dram_tensor("ck_all", [2, 2, NP, 128, NBLK], F32, kind="Internal").ap()
    x1_sp = nc.dram_tensor("x1_sp", [T, DIM], F32,
                           kind="ExternalOutput" if debug else "Internal").ap()
    if debug:
        xha_dbg = nc.dram_tensor("xha_dbg", [128, NF * RN], BF16, kind="ExternalOutput").ap()
        cko_dbg = nc.dram_tensor("cko_dbg", [2, NP, 128, NBLK], F32, kind="ExternalOutput").ap()
        st_dbg = nc.dram_tensor("st_dbg", [128, 64], F32, kind="ExternalOutput").ap()
        cw_dbg = nc.dram_tensor("cw_dbg", [128, 16 * 96], F32, kind="ExternalOutput").ap()
        ckt_dbg = nc.dram_tensor("ckt_dbg", [2, 128, NP * 2 * NCK], BF16, kind="ExternalOutput").ap()

    sem_names = ["qc", "qw", "qsl", "pe", "dve", "act", "cc",
                 "qx0", "qx1", "kx0", "kx1", "kc0", "kc1", "kw0", "kw1",
                 "qw0", "qw1", "cko", "qld", "qdb", "sA0", "sA1", "sB0", "sB1",
                 "sC0", "sC1", "sD0", "sD1", "out0", "out1", "pool"]

    import contextlib
    ctx = contextlib.ExitStack()
    with ctx:
        sems = {s: ctx.enter_context(nc.semaphore(s)) for s in sem_names}

        pers = contextlib.ExitStack()
        sb = lambda name, shape, dt=BF16: pers.enter_context(nc.sbuf_tensor(name, list(shape), dt))
        with pers:
            ident = sb("ident", (128, 128))
            ident32 = sb("ident32", (128, 128), F32)
            rmsw_s = sb("rmsw_s", (128, NF), F32)
            scales_g = sb("scales_g", (128, NI), F32)
            scales_u = sb("scales_u", (128, NI), F32)
            scales_d = sb("scales_d", (128, NF), F32)
            eps4_c = sb("eps4_c", (128, 1), F32)
            ph13 = contextlib.ExitStack()
            sb13 = lambda name, shape, dt=BF16: ph13.enter_context(nc.sbuf_tensor(name, list(shape), dt))
            cw_tm = sb13("cw_tm", (128, 16 * 96), F32)
            ck_sb = sb13("ck_sb", (128, NP * 2 * NCK))
            cvt_sb = sb13("cvt_sb", (128, H * 64))
            lmask = sb13("lmask", (128, 2 * 2 * NKK))
            cmask = sb13("cmask", (128, 16 * 2 * NCK))

            t_ep4 = p.emit("vector", lambda e: e.memset(eps4_c[:], EPS_LN), inc="dve")
            p.dma("sync", "qc", ident[:], ident_h[:, :])
            p.dma("sync", "qc", ident32[:], ident32_h[:, :])
            p.dma("sync", "qc", rmsw_s[:], rmsw_c[:, :])
            p.dma("sync", "qc", lmask[:].rearrange("p (s n) -> p s n", s=2),
                  lmask_h.rearrange("s p n -> p s n"))
            p.dma("sync", "qc", cmask[:].rearrange("p (s n) -> p s n", s=16),
                  cmask_h.rearrange("s p n -> p s n"))
            p.dma("sync", "qc", scales_g[:], gs_c[:, :])
            p.dma("sync", "qc", scales_u[:], us_c[:, :])
            p.dma("sync", "qc", scales_d[:], ds_c[:, :])

            ph1 = contextlib.ExitStack()
            sb1 = lambda name, shape, dt=BF16: ph1.enter_context(nc.sbuf_tensor(name, list(shape), dt))
            ps1 = lambda name, shape, dt=F32: ph1.enter_context(nc.psum_tensor(name, list(shape), dt))
            with ph1:
                xha = sb1("xha", (128, NF * RN))
                xt = [sb1(f"xt{i}", (128, DIM), F32) for i in range(2)]
                sqt = sb1("sqt", (128, DIM), F32)
                xhtm = [sb1(f"xhtm{i}", (128, DIM)) for i in range(2)]
                cols = sb1("cols", (128, 16), F32)
                ssum = sb1("ssum", (128, NF + 1), F32)
                ssq = sb1("ssq", (128, NF + 1), F32)
                stm = sb1("stm", (128, NF + 1), F32)
                stq = sb1("stq", (128, NF + 1), F32)
                stt = sb1("stt", (128, NF + 1), F32)
                stv = sb1("stv", (128, NF + 1), F32)
                str1 = sb1("str1", (128, NF + 1), F32)
                stS = sb1("stS", (128, NF + 1), F32)
                stnm = sb1("stnm", (128, NF + 1), F32)
                wslab = [sb1(f"wslab{i}", (128, DIM // 2)) for i in range(NF)]
                kx = [sb1(f"kx{i}", (128, RN + CBS)) for i in range(2)]
                kcp = [sb1(f"kcp{i}", (128, CBS * 64)) for i in range(2)]
                qstage = [sb1(f"qstage{i}", (128, 512)) for i in range(2)]
                ckst = [sb1(f"ckst{i}", (128, NBLK), F32) for i in range(2)]
                kcbs = sb1("kcbs", (128, 2 * NP), F32)
                bcomb_bc = sb1("bcomb_bc", (128, 96), F32)
                bcomb_s = sb1("bcomb_s", (1, 96))
                cwtmp = sb1("cwtmp", (128, 96), F32)
                stcap = sb1("stcap", (128, 64), F32) if debug else None

                epsln_c = sb1("epsln_c", (128, 1), F32)
                epsrm_c = sb1("epsrm_c", (128, 1), F32)
                vtst = sb1("vtst", (128, (NF + 1) * 128))
                pst = [ps1(f"pst{i}", (128, 128), BF16) for i in range(2)]
                ps_vt = ps1("ps_vt", (128, 1024), BF16)
                psb = [ps1(f"psb{i}", (128, 512)) for i in range(2)]
                psc0 = ps1("psc0", (64, NBLK + 1))
                psc1 = ps1("psc1", (128, NBLK + 1))
                pscb = ps1("pscb", (128, 96))

                p.dma("sync", "qc", kcbs[:], kcb_l[:, :])
                t_qc = ("qc", p.cnt["qc"], "sync", True)
                t_id = t_rw = t_kcb = t_qc
                t_ep1 = p.emit("vector", lambda e: e.memset(epsln_c[:], EPS_LN), inc="dve")
                t_ep2 = p.emit("vector", lambda e: e.memset(epsrm_c[:], EPS_RMS), inc="dve")
                p.dma("gpsimd", "qw", bcomb_s[:], bcomb[:, :])

                xrot, trot = Rot(2), Rot(2)
                NTC = NF + 1
                for tc in range(NTC):
                    s, wt = xrot.acquire()
                    t_x = p.dma("sync", f"qx{s}", xt[s][:],
                                x_in[tc * 128:(tc + 1) * 128, :], waits=wt)
                    v = "vector"
                    p.emit(v, lambda e, s=s: e.tensor_mul(sqt[:], xt[s][:], xt[s][:]),
                           waits=[t_x])
                    p.emit(v, lambda e, s=s, tc=tc: e.reduce_sum(
                        ssum[:, tc:tc + 1], xt[s][:], AX.X))
                    t_r = p.emit(v, lambda e, tc=tc: e.reduce_sum(
                        ssq[:, tc:tc + 1], sqt[:], AX.X), inc="dve")
                    xrot.release(s, t_r)
                v = "vector"
                SP = lambda: p.emit(v, lambda e: e.tensor_copy(cols[:, 15:16],
                                                               ssum[:, 0:1]))
                SP(); SP(); SP()
                p.emit(v, lambda e: e.tensor_scalar_mul(stm[:], ssum[:], 1.0 / DIM))
                p.emit(v, lambda e: e.tensor_scalar_mul(stq[:], ssq[:], 1.0 / DIM))
                SP(); SP()
                p.emit(v, lambda e: e.tensor_mul(stt[:], stm[:], stm[:]))
                SP(); SP()
                t_var = p.emit(v, lambda e: e.tensor_sub(stv[:], stq[:], stt[:]),
                               inc="dve")
                t_s1 = p.emit("scalar", lambda e: e.activation(
                    stt[:], stv[:], AF.Sqrt, bias=epsln_c[:, :]),
                    waits=[t_var, t_ep1], inc="act")
                p.emit(v, lambda e: e.reciprocal(str1[:], stt[:]), waits=[t_s1])
                SP(); SP(); SP()
                p.emit(v, lambda e: e.tensor_mul(stt[:], stv[:], str1[:]))
                SP(); SP()
                t_b2 = p.emit(v, lambda e: e.tensor_mul(stq[:], stt[:], str1[:]),
                              inc="dve")
                t_s2 = p.emit("scalar", lambda e: e.activation(
                    stt[:], stq[:], AF.Sqrt, bias=epsrm_c[:, :]),
                    waits=[t_b2, t_ep2], inc="act")
                p.emit(v, lambda e: e.reciprocal(stq[:], stt[:]), waits=[t_s2])
                SP(); SP(); SP()
                p.emit(v, lambda e: e.tensor_mul(stS[:], str1[:], stq[:]))
                SP(); SP()
                p.emit(v, lambda e: e.tensor_mul(stnm[:], stm[:], stS[:]))
                SP(); SP()
                t_stats = p.emit(v, lambda e: e.tensor_scalar_mul(stnm[:], stnm[:], -1.0),
                                 inc="dve")
                SP(); SP(); SP()
                if debug:
                    p.emit(v, lambda e: e.tensor_copy(stcap[:, 0:NTC], stm[:]))
                    p.emit(v, lambda e: e.tensor_copy(stcap[:, 32:32 + NTC], stS[:]))
                xa_done = []
                xh_reader = [None, None]
                for tc in range(NTC):
                    s, wt = xrot.acquire()
                    t_x = p.dma("sync", f"qx{s}", xt[s][:],
                                x_in[tc * 128:(tc + 1) * 128, :], waits=wt)
                    hs = tc % 2
                    t_xh = p.emit(v, lambda e, s=s, hs=hs, tc=tc: e.tensor_scalar(
                        xhtm[hs][:], xt[s][:], stS[:, tc:tc + 1], stnm[:, tc:tc + 1],
                        ALU.mult, ALU.add),
                        waits=[t_x, t_stats, xh_reader[hs]], inc="dve")
                    xrot.release(s, t_xh)
                    for f in range(NF):
                        ts, twt = trot.acquire()
                        t_tp = p.emit("tensor", lambda e, f=f, hs=hs, ts=ts: e.transpose(
                            pst[ts][:, :], xhtm[hs][:, f * 128:(f + 1) * 128],
                            ident[:, :]), waits=twt + [t_xh, t_id], inc="pe")
                        if f == NF - 1:
                            xh_reader[hs] = t_tp
                        t_ev = p.emit(v, lambda e, f=f, tc=tc, ts=ts: e.tensor_scalar_mul(
                            xha[:, f * RN + tc * 128: f * RN + (tc + 1) * 128],
                            pst[ts][:, :], rmsw_s[:, f:f + 1]),
                            waits=[t_tp, t_rw], inc="dve")
                        trot.release(ts, t_ev)
                        if f == NF - 1:
                            xa_done.append(t_ev)
                xa_all = xa_done[-1]

                if debug:
                    p.dma("gpsimd", "qdb", xha_dbg[:, :], xha[:], waits=[xa_all])
                    p.dma("gpsimd", "qdb", st_dbg[:, :], stcap[:], waits=[xa_all])

                brot = Rot(2)
                qsrot = Rot(2)
                kxrot = Rot(2)
                kcrot = Rot(2)
                ckrot = Rot(2)
                slab_tok = {}
                psc_guard = [[], []]

                def qkv_group(grp, prev_pe):
                    is_q = (grp == 0)
                    coff = E if is_q else 0
                    nch = [(i * 512, 512) for i in range(4)] + ([] if is_q else [(2048, 128)])
                    spill = [q_sp, k_sp, v_sp][grp]
                    last_pe = prev_pe
                    qtr_pe = [prev_pe, prev_pe]
                    for jh in range(4):
                      jo = jh * (NF // 4)
                      qsl_s = jh % 2
                      for kc in range(NF):
                        p.dma(
                            "gpsimd", "qsl",
                            wslab[kc][:, qsl_s * 512:(qsl_s + 1) * 512],
                            wqkv[kc * 128:(kc + 1) * 128,
                                 grp * DIM + jh * (DIM // 4):grp * DIM + (jh + 1) * (DIM // 4)],
                            waits=[qtr_pe[qsl_s]] if kc == 0 else [])
                      t_slabs = ("qsl", p.cnt["qsl"], "gpsimd", True)
                      for jr in range(NF // 4):
                        j = jo + jr
                        if not is_q:
                            ks, kwt = kcrot.acquire()
                            t_kc = p.dma("gpsimd", f"kc{ks}", kcp[ks][:].rearrange(
                                "p (t o) -> p t o", t=CBS),
                                kcw_l[j, grp - 1, :, :, :], waits=kwt)
                            xs, xwt = kxrot.acquire()
                            t_kp = p.dma("gpsimd", f"kx{xs}", kx[xs][:, RN:RN + CBS],
                                         kpos_l[j, grp - 1, :, :], waits=xwt)
                        ev_toks = []
                        for (c0, cw) in nch:
                            bs, bwt = brot.acquire()
                            mm = None
                            for kc in range(NF):
                                wts = [t_slabs, xa_all] + (bwt if kc == 0 else [])
                                tk = p.emit("tensor", lambda e, kc=kc, jr=jr, bs=bs, c0=c0, cw=cw, qsl_s=qsl_s: e.matmul(
                                    psb[bs][:, 0:cw],
                                    wslab[kc][:, qsl_s * 512 + jr * 128:
                                               qsl_s * 512 + (jr + 1) * 128],
                                    xha[:, kc * RN + coff + c0: kc * RN + coff + c0 + cw],
                                    start=(kc == 0), stop=(kc == NF - 1)),
                                    waits=wts, inc="pe" if kc == NF - 1 else None)
                                if tk is not None:
                                    mm = tk
                            last_pe = mm
                            if is_q:
                                ss, swt = qsrot.acquire()
                                t_ev = p.emit("vector", lambda e, bs=bs, ss=ss, cw=cw: e.tensor_scalar_mul(
                                    qstage[ss][:, 0:cw], psb[bs][:, 0:cw], SCALE),
                                    waits=[mm] + swt, inc="dve")
                                t_st = p.dma("gpsimd", f"qw{ss}", spill[j, :, c0:c0 + cw],
                                             qstage[ss][:, 0:cw], waits=[t_ev])
                                qsrot.release(ss, t_st)
                            else:
                                t_ev = p.emit("vector", lambda e, bs=bs, xs=xs, c0=c0, cw=cw: e.tensor_copy(
                                    kx[xs][:, c0:c0 + cw], psb[bs][:, 0:cw]),
                                    waits=[mm] + (xwt if c0 == 0 else []), inc="dve")
                                ev_toks.append(t_ev)
                            brot.release(bs, t_ev)
                        if is_q:
                            continue
                        if grp == 1:
                            t_sp = p.dma("sync", f"kw{xs}", spill[j, :, 0:RN],
                                         kx[xs][:, 0:RN], waits=ev_toks)
                        else:
                            cpy = None
                            for tcx in range(NF + 1):
                                bidx = tcx % 8
                                wt2 = (ev_toks if tcx == 0 else [])
                                if bidx == 0:
                                    wt2 = wt2 + [cpy]
                                t_tp = p.emit("tensor", lambda e, tcx=tcx, bidx=bidx, xs=xs: e.transpose(
                                    ps_vt[:, bidx * 128:(bidx + 1) * 128],
                                    kx[xs][:, tcx * 128:(tcx + 1) * 128], ident[:, :]),
                                    waits=wt2 + [t_id],
                                    inc="pe" if (bidx == 7 or tcx == NF) else None)
                                if bidx == 7 or tcx == NF:
                                    nb = bidx + 1
                                    cpy = p.emit("vector", lambda e, tcx=tcx, nb=nb: e.tensor_copy(
                                        vtst[:, (tcx - nb + 1) * 128:(tcx + 1) * 128],
                                        ps_vt[:, 0:nb * 128]), waits=[t_tp], inc="dve")
                            t_sp = p.dma(
                                "sync", f"kw{xs}",
                                v_sp[:, :, j * 128:(j + 1) * 128].rearrange("w p f -> p w f"),
                                vtst[:].rearrange("p (w f) -> p w f", w=NF + 1),
                                waits=[cpy])
                            last_pe = t_tp
                        lastmm = None
                        for t in range(CBS):
                            st_, sp_ = (t == 0), (t == CBS - 1)
                            wts = ([t_kc, t_kp] + ev_toks + psc_guard[0] + psc_guard[1]) \
                                if t == 0 else []
                            p.emit("tensor", lambda e, t=t, ks=ks, xs=xs, st_=st_, sp_=sp_: e.matmul(
                                psc0[:, :], kcp[ks][0:64, t * 64:(t + 1) * 64],
                                kx[xs][0:64, E + t: E + t + 64 * NBLK + 1: 64],
                                start=st_, stop=sp_), waits=wts)
                            lastmm = p.emit("tensor", lambda e, t=t, ks=ks, xs=xs, st_=st_, sp_=sp_: e.matmul(
                                psc1[64:128, :], kcp[ks][64:128, t * 64:(t + 1) * 64],
                                kx[xs][64:128, E + t: E + t + 64 * NBLK + 1: 64],
                                start=st_, stop=sp_, tile_position=(64, 64)),
                                inc="pe" if sp_ else None)
                        last_pe = lastmm
                        kcrot.release(ks, lastmm)
                        cs, cwt = ckrot.acquire()
                        kcol = (grp - 1) * NP + j
                        t_e0 = p.emit("vector", lambda e, cs=cs, kcol=kcol: e.tensor_scalar(
                            ckst[cs][0:64, :], psc0[0:64, 0:NBLK], psc0[0:64, NBLK:NBLK + 1],
                            kcbs[0:64, kcol:kcol + 1], ALU.add, ALU.add),
                            waits=[lastmm, t_kcb] + cwt, inc="dve")
                        t_e1 = p.emit("vector", lambda e, cs=cs, kcol=kcol: e.tensor_scalar(
                            ckst[cs][64:128, :], psc1[64:128, 0:NBLK], psc1[64:128, NBLK:NBLK + 1],
                            kcbs[64:128, kcol:kcol + 1], ALU.add, ALU.add), inc="dve")
                        psc_guard[0] = [t_e0]
                        psc_guard[1] = [t_e1]
                        t_cko = p.dma("gpsimd", f"sA{cs}", ck_own[grp - 1, j, :, :], ckst[cs][:],
                                      waits=[t_e0, t_e1])
                        ckrot.release(cs, t_cko)
                        kxrot.release(xs, t_sp)
                        kxrot.release(xs, last_pe)
                    return last_pe

                g_pe = qkv_group(0, None)

                wcomb_sb = sb1("wcomb_sb", (128, NF * 96))
                p.dma("gpsimd", "qw", wcomb_sb[:].rearrange("p (k c) -> p k c", k=NF),
                      wcomb.rearrange("(k p) c -> p k c", p=128))
                t_wc = t_bc = ("qw", p.cnt["qw"], "gpsimd", True)
                ones_sb = sb1("ones_sb", (1, 128))
                t_on = p.emit("vector", lambda e: e.memset(ones_sb[:], 1.0), inc="dve")
                t_bb = p.emit("tensor", lambda e: e.matmul(
                    pscb[:, :], ones_sb[:, :], bcomb_s[:, :], start=True, stop=True),
                    waits=[t_bc, t_on], inc="pe")
                t_bbe = p.emit("vector", lambda e: e.tensor_copy(bcomb_bc[:], pscb[:]),
                               waits=[t_bb], inc="dve")
                prev_ad = t_bbe
                prev_sig = None
                for tc in range(16):
                    mm = None
                    for kc in range(NF):
                        tk = p.emit("tensor", lambda e, tc=tc, kc=kc: e.matmul(
                            pscb[:, :],
                            xha[:, kc * RN + E + tc * 128: kc * RN + E + (tc + 1) * 128],
                            wcomb_sb[:, kc * 96:(kc + 1) * 96],
                            start=(kc == 0), stop=(kc == NF - 1)),
                            waits=[prev_ad, t_wc, xa_all] if kc == 0 else [],
                            inc="pe" if kc == NF - 1 else None)
                        if tk is not None:
                            mm = tk
                    t_ad = p.emit("vector", lambda e, tc=tc: e.tensor_add(
                        cwtmp[:], pscb[:], bcomb_bc[:]), waits=[mm, prev_sig], inc="dve")
                    prev_ad = t_ad
                    prev_sig = p.emit("scalar", lambda e, tc=tc: e.activation(
                        cw_tm[:, tc * 96:(tc + 1) * 96], cwtmp[:], AF.Sigmoid),
                        waits=[t_ad], inc="act")

                g_pe = qkv_group(1, g_pe)
                g_pe = qkv_group(2, g_pe)

                all_ck = [("sA0", p.cnt.get("sA0", 0), "gpsimd", True),
                          ("sA1", p.cnt.get("sA1", 0), "gpsimd", True)]
                t_cc = p.emit("gpsimd", lambda e: e.collective_compute(
                    "AllGather", ALU.bypass,
                    replica_groups=[[0, 1], [2, 3], [4, 5], [6, 7]],
                    ins=[ck_own.rearrange("kv g p n -> (kv g p) n")],
                    outs=[ck_all.rearrange("r kv g p n -> (r kv g p) n")]),
                    waits=all_ck, inc="cc")
                t_ccw = ("cc", p.cnt["cc"], "gpsimd", True)

                cv_fm = sb1("cv_fm", (128, NP * NCK))
                t_zk = p.emit("vector", lambda e: e.memset(ck_sb[:], 0.0), inc="dve")
                for hb in range(2):
                    ckv = ck_sb[hb * 64:(hb + 1) * 64].rearrange(
                        "p (g n) -> p g n", g=NP)[:, :, hb * NCK:hb * NCK + NCK]
                    for r in range(2):
                        p.dma("gpsimd", "qld",
                              ckv[:, :, 4 + r * NBLK: 4 + (r + 1) * NBLK],
                              ck_all[r, 0, :, hb * 64:(hb + 1) * 64, :].rearrange(
                                  "g p n -> p g n"),
                              waits=[t_ccw, t_zk])
                    p.dma("gpsimd", "qld", ckv[:, :, 0:4],
                          mem_l[0, :, hb * 64:(hb + 1) * 64, :].rearrange("g p n -> p g n"))
                for r in range(2):
                    p.dma("gpsimd", "qld",
                          cv_fm[:].rearrange("p (g n) -> p g n", g=NP)[
                              :, :, 4 + r * NBLK: 4 + (r + 1) * NBLK],
                          ck_all[r, 1].rearrange("g p n -> p g n"), waits=[t_ccw])
                p.dma("gpsimd", "qld",
                      cv_fm[:].rearrange("p (g n) -> p g n", g=NP)[:, :, 0:4],
                      mem_l[1].rearrange("g p n -> p g n"))
                t_ckl = ("qld", p.cnt["qld"], "gpsimd", True)
                t_z = p.emit("vector", lambda e: e.memset(cvt_sb[:], 0.0), inc="dve")
                for h in range(H):
                    ts, twt = trot.acquire()
                    b0 = (h % 2) * 64
                    t_tp = p.emit("tensor", lambda e, h=h, ts=ts, b0=b0: e.transpose(
                        pst[ts][0:NCK, 0:64],
                        cv_fm[b0:b0 + 64,
                              (h // 2) * NCK:(h // 2 + 1) * NCK],
                        ident[b0:b0 + 64, b0:b0 + 64]), waits=twt + [t_ckl], inc="pe")
                    t_ev = p.emit("vector", lambda e, h=h, ts=ts: e.tensor_copy(
                        cvt_sb[0:NCK, h * 64:(h + 1) * 64], pst[ts][0:NCK, 0:64]),
                        waits=[t_tp, t_z], inc="dve")
                    trot.release(ts, t_ev)

                if debug:
                    p.dma("gpsimd", "qdb", cko_dbg.rearrange("a b c d -> (a b c) d"),
                          ck_own.rearrange("a b c d -> (a b c) d"))
                    p.dma("gpsimd", "qdb", cw_dbg[:, :], cw_tm[:], waits=[("act", p.cnt["act"], "scalar", False)])
                    p.dma("gpsimd", "qdb", ckt_dbg[0, :, 0:NP * 2 * NCK], ck_sb[:])
                    p.dma("gpsimd", "qdb", ckt_dbg[1, :, 0:H * 64], cvt_sb[:])

                p1_join = [(s, p.cnt[s], "none", True) for s in p.cnt]

                with nc.Block() as blk:
                    p.replay(blk, sems)

            t_spq = [("qw0", p.cnt.get("qw0", 0), "g", True),
                     ("qw1", p.cnt.get("qw1", 0), "g", True)]
            t_spk = [("kw0", p.cnt.get("kw0", 0), "g", True),
                     ("kw1", p.cnt.get("kw1", 0), "g", True)]
            ph3 = contextlib.ExitStack()
            sb3 = lambda name, shape, dt=BF16: ph3.enter_context(nc.sbuf_tensor(name, list(shape), dt))
            ps3 = lambda name, shape, dt=F32: ph3.enter_context(nc.psum_tensor(name, list(shape), dt))
            with ph3:
                qsbd = [sb3("q_sbA", (128, NF * 512)),
                        sb3("q_sbB", (128, NF * 512))]
                vtm = sb3("vtm", (128, 5 * DIM))
                att = sb3("att", (128, NF * 512))
                wpad = [sb3(f"wpad{i}", (128, 2 * 640)) for i in range(2)]
                wout_sb = [sb3(f"wos{i}", (128, DIM)) for i in range(NF)]
                P_sb = [sb3(f"Pp{i}", (128, 648)) for i in range(3)]
                Pm_sb = [sb3(f"Pm{i}", (128, 648)) for i in range(3)]
                PT_sb = [sb3(f"PTp{i}", (128, 768)) for i in range(3)]
                y3_sb = [sb3(f"y3{i}", (128, 512), F32) for i in range(2)]
                xres3 = sb3("xres3", (128, 4 * 1024), F32)
                x1st3 = sb3("x1st3", (128, 4 * 1024), F32)
                lcol_ = [sb3(f"lcol{i}", (128, 16), F32) for i in range(3)]
                gcol_ = [sb3(f"gcol{i}", (128, 16), F32) for i in range(3)]
                spc3 = sb3("spc3", (128, 2), F32)

                ps_S = [ps3(f"ps_S{i}", (128, 512)) for i in range(2)]
                ps_C = [ps3(f"ps_C{i}", (128, 136)) for i in range(2)]
                ps_T = [ps3("ps_T0", (128, 768), BF16)]
                ps_O = [ps3(f"ps_O{i}", (128, 128)) for i in range(2)]
                ps_W = ps3("ps_W", (128, 512))

                p.set_barrier(p.engine_totals())
                for kc in range(NF):
                    p.dma("gpsimd", "qsl", wout_sb[kc][:],
                          wout[kc * 128:(kc + 1) * 128, :])
                t_wout = ("qsl", p.cnt["qsl"], "g", True)
                for dst, src_ in ((gate_bf, gateT), (up_bf, upT), (down_bf, downT)):
                    for hf in range(4):
                        r0, r1 = hf * (dst.shape[0] // 4), (hf + 1) * (dst.shape[0] // 4)
                        p.dma("gpsimd", "qw", dst[r0:r1, :], src_[r0:r1, :])
                t_wbf = ("qw", p.cnt["qw"], "g", True)
                t_z1 = p.emit("vector", lambda e: e.memset(wpad[0][:], 0.0), inc="dve")
                t_z2 = p.emit("vector", lambda e: e.memset(wpad[1][:], 0.0), inc="dve")
                t_z3 = None
                for i3 in range(3):
                    t_z3 = p.emit("vector", lambda e, i3=i3: e.memset(
                        PT_sb[i3][64:128, 512:768], 0.0), inc="dve")

                rotS = Rot(2)
                rotP = Rot(3)
                wpr = Rot(2)
                yrot = Rot(2)
                ytr = Rot(2)
                oro = Rot(2)
                ptr = Rot(1)
                prev_q_user = None
                prev_v_user = None
                prev_att_user = None
                prev_wevac = None
                prev_x1_user = [None, None]
                prev_store = [[], []]
                SP3 = lambda: p.emit("vector", lambda e: e.tensor_copy(spc3[:, 0:1], lcol[:, 0:1]))
                def qload(cc, wl):
                    p.dma("sync", "sB0",
                          qsbd[cc % 2][:].rearrange("p (f t) -> p f t", f=NF),
                          q_sp[:, :, cc * 512:(cc + 1) * 512].rearrange("f p t -> p f t"),
                          waits=wl)
                    return ("sB0", p.cnt["sB0"], "g", True)
                qtok = {0: qload(0, t_spq)}
                for c in range(4):
                    t_q = qtok[c]
                    qfree = prev_q_user
                    vprev = prev_v_user
                    t_v = None
                    last_pe3 = None
                    pendq = []

                    def emit_tail(it):
                        nonlocal last_pe3, prev_att_user
                        pP, g, qc, tc, t_exp = it
                        lcol = lcol_[pP]
                        gcol = gcol_[pP]
                        cwb = tc * 96 + 6 * g
                        t_r1 = p.emit("vector", lambda e, lcol=lcol: e.reciprocal(
                            lcol[:, 4:6], lcol[:, 0:2]),
                            waits=[t_exp])
                        p.emit("vector", lambda e, lcol=lcol: e.reciprocal(
                            lcol[:, 6:8], lcol[:, 2:4]))
                        p.emit("vector", lambda e, cwb=cwb, gcol=gcol: e.tensor_add(
                            gcol[:, 4:6], cw_tm[:, cwb:cwb + 4:3],
                            cw_tm[:, cwb + 1:cwb + 5:3]))
                        SP3(); SP3()
                        p.emit("vector", lambda e, gcol=gcol, lcol=lcol: e.tensor_mul(
                            gcol[:, 6:8], gcol[:, 4:6], lcol[:, 4:6]))
                        t_gt = p.emit("vector", lambda e, cwb=cwb, gcol=gcol, lcol=lcol: e.tensor_mul(
                            gcol[:, 8:10], cw_tm[:, cwb + 2:cwb + 6:3],
                            lcol[:, 6:8]), inc="dve")
                        t_poolS = None
                        for hb in range(2):
                            t_poolS = p.emit("vector", lambda e, pP=pP, hb=hb, gcol=gcol: e.tensor_scalar_mul(
                                Pm_sb[pP][:, hb * 256:(hb + 1) * 256],
                                P_sb[pP][:, hb * 256:(hb + 1) * 256],
                                gcol[:, 6 + hb:7 + hb]),
                                waits=[t_gt] if hb == 0 else [],
                                inc="dve" if hb == 1 else None)
                        t_poolC = None
                        for hb in range(2):
                            t_poolC = p.emit("vector", lambda e, pP=pP, hb=hb, gcol=gcol: e.tensor_scalar_mul(
                                Pm_sb[pP][:, 512 + hb * 68:512 + (hb + 1) * 68],
                                P_sb[pP][:, 512 + hb * 68:512 + (hb + 1) * 68],
                                gcol[:, 8 + hb:9 + hb]),
                                inc="dve" if hb == 1 else None)
                        pts, ptw = ptr.acquire()
                        tt = None
                        for k6 in range(4):
                            tt = p.emit("tensor", lambda e, k6=k6, pP=pP: e.transpose(
                                ps_T[0][:, k6 * 128:(k6 + 1) * 128],
                                Pm_sb[pP][:, k6 * 128:(k6 + 1) * 128], ident[:, :]),
                                waits=[t_poolS, t_poolC] + (ptw if k6 == 0 else []))
                        for hb in range(2):
                            tt = p.emit("tensor", lambda e, hb=hb, pP=pP: e.transpose(
                                ps_T[0][0:NCK, 512 + hb * 128:512 + hb * 128 + 128],
                                Pm_sb[pP][:, 512 + hb * 68:512 + (hb + 1) * 68],
                                ident[:, :]),
                                inc="pe" if hb == 1 else None)
                        p.emit("vector", lambda e, pP=pP: e.tensor_copy(
                            PT_sb[pP][:, 0:512], ps_T[0][:, 0:512]),
                            waits=[tt, t_z3])
                        t_ptev = p.emit("vector", lambda e, pP=pP: e.tensor_copy(
                            PT_sb[pP][0:NCK, 512:768], ps_T[0][0:NCK, 512:768]),
                            inc="dve")
                        ptr.release(pts, t_ptev)
                        so, sow = oro.acquire()
                        lastO = None
                        for hb in range(2):
                            for st3 in range(3):
                                if st3 < 2:
                                    lhs = vtm[:, (qc + st3) * DIM + (2 * g + hb) * 64:
                                              (qc + st3) * DIM + (2 * g + hb) * 64 + 64]
                                    rhs = PT_sb[pP][:, hb * 256 + st3 * 128:
                                                    hb * 256 + (st3 + 1) * 128]
                                else:
                                    lhs = cvt_sb[:, (2 * g + hb) * 64:(2 * g + hb + 1) * 64]
                                    rhs = PT_sb[pP][:, 512 + hb * 128:512 + (hb + 1) * 128]
                                lastO = p.emit("tensor", lambda e, lhs=lhs, rhs=rhs, hb=hb,
                                               st3=st3, so=so: e.matmul(
                                    ps_O[so][hb * 64:(hb + 1) * 64, :], lhs, rhs,
                                    start=(st3 == 0), stop=(st3 == 2),
                                    tile_position=(0, hb * 64)),
                                    waits=([t_ptev, t_v, t_ckl] + sow
                                           if (hb == 0 and st3 == 0) else []),
                                    inc="pe" if (hb == 1 and st3 == 2) else None)
                        t_oev = p.emit("scalar", lambda e, g=g, qc=qc, so=so: e.copy(
                            att[:, g * 512 + qc * 128: g * 512 + (qc + 1) * 128],
                            ps_O[so][:, :]), waits=[lastO], inc="act")
                        oro.release(so, t_oev)
                        prev_att_user = t_oev
                        last_pe3 = lastO
                        rotP.release(pP, t_ptev)
                        rotP.release(pP, lastO)
                        rotP.release(pP, t_poolS)
                        rotP.release(pP, t_oev)

                    for g in range(NP):
                        ws, wwt = wpr.acquire()
                        p.dma("sync", f"sC{ws}", wpad[ws][0:64, 0:640],
                              k_sp[g, 0:64, c * 512:c * 512 + 640],
                              waits=t_spk + wwt + [t_z1 if ws == 0 else t_z2])
                        p.dma("sync", f"sC{ws}", wpad[ws][64:128, 640:1280],
                              k_sp[g, 64:128, c * 512:c * 512 + 640])
                        t_wp = (f"sC{ws}", p.cnt[f"sC{ws}"], "g", True)
                        if g == 0:
                            if c + 1 < 4:
                                qtok[c + 1] = qload(c + 1, t_spq + [qfree])
                            p.dma("sync", "sD0",
                                  vtm[:].rearrange("p (w f) -> p w f", w=5),
                                  v_sp[c * 4:c * 4 + 5, :, :].rearrange("w p f -> p w f"),
                                  waits=t_spk + [vprev])
                            t_v = ("sD0", p.cnt["sD0"], "g", True)
                        for qc in range(4):
                            tc = c * 4 + qc
                            sS, sw = rotS.acquire()
                            pP, pw = rotP.acquire()
                            lcol = lcol_[pP]
                            gcol = gcol_[pP]
                            lq = qsbd[c % 2][:, g * 512 + qc * 128: g * 512 + (qc + 1) * 128]
                            rhsS = wpad[ws].rearrange("p (g2 w) -> p g2 w", g2=2)[
                                :, :, qc * 128:qc * 128 + 256]
                            p.emit("tensor", lambda e, sS=sS, lq=lq, rhsS=rhsS: e.matmul(
                                ps_S[sS][:, :], lq, rhsS, start=True, stop=False),
                                waits=[t_q, t_wp] + sw)
                            p.emit("tensor", lambda e, sS=sS, lq=lq, g=g: e.matmul(
                                ps_C[sS][:, :], lq,
                                ck_sb[:, g * 2 * NCK:(g + 1) * 2 * NCK],
                                start=True, stop=False), waits=[t_ckl])
                            slot = 0 if tc == 0 else 1
                            p.emit("tensor", lambda e, sS=sS, slot=slot: e.matmul(
                                ps_S[sS][:, :], ident[:, :],
                                lmask[:, slot * 512:(slot + 1) * 512],
                                start=False, stop=True), waits=[t_qc, t_id])
                            t_mm = p.emit("tensor", lambda e, sS=sS, tc=tc: e.matmul(
                                ps_C[sS][:, :], ident[:, :],
                                cmask[:, tc * 136:(tc + 1) * 136],
                                start=False, stop=True), inc="pe")
                            t_nm = p.emit("vector", lambda e, sS=sS, gcol=gcol: e.reduce_max(
                                gcol[:, 0:2],
                                ps_S[sS][:].rearrange("p (h n) -> p h n", h=2),
                                AX.X, negate=True),
                                waits=[t_mm] + sw + pw, inc="dve")
                            t_nc2 = p.emit("vector", lambda e, sS=sS, gcol=gcol: e.reduce_max(
                                gcol[:, 2:4],
                                ps_C[sS][:].rearrange("p (h n) -> p h n", h=2),
                                AX.X, negate=True), inc="dve")
                            ee = []
                            for hb in range(2):
                                ee.append(p.emit("scalar", lambda e, sS=sS, pP=pP, hb=hb, gcol=gcol, lcol=lcol: e.activation(
                                    P_sb[pP][:, hb * 256:(hb + 1) * 256],
                                    ps_S[sS][:, hb * 256:(hb + 1) * 256], AF.Exp,
                                    bias=gcol[:, hb:hb + 1],
                                    accum_out=lcol[:, hb:hb + 1]),
                                    waits=[t_nm, t_nc2], inc="act"))
                            for hb in range(2):
                                ee.append(p.emit("scalar", lambda e, sS=sS, pP=pP, hb=hb, gcol=gcol, lcol=lcol: e.activation(
                                    P_sb[pP][:, 512 + hb * 68:512 + (hb + 1) * 68],
                                    ps_C[sS][:, hb * 68:(hb + 1) * 68], AF.Exp,
                                    bias=gcol[:, 2 + hb:3 + hb],
                                    accum_out=lcol[:, 2 + hb:3 + hb]), inc="act"))
                            t_exp = ee[-1]
                            rotS.release(sS, ee[1])
                            rotS.release(sS, t_nm)
                            rotS.release(sS, t_nc2)
                            rotS.release(sS, ee[3])
                            pendq.append((pP, g, qc, tc, t_exp))
                            if len(pendq) > 2:
                                emit_tail(pendq.pop(0))
                        wpr.release(ws, last_pe3)
                    while pendq:
                        emit_tail(pendq.pop(0))
                    prev_q_user = last_pe3
                    prev_v_user = last_pe3
                    for dh2 in range(2):
                        t_xr = None
                        for tc4 in range(4):
                            t_xr = p.dma("gpsimd", "sA0",
                                         xres3[:, tc4 * 1024:(tc4 + 1) * 1024],
                                         x_in[E + c * 512 + tc4 * 128:
                                              E + c * 512 + (tc4 + 1) * 128,
                                              dh2 * 1024:(dh2 + 1) * 1024],
                                         waits=[prev_x1_user[0]] if tc4 == 0 else [])
                        t_xr = ("sA0", p.cnt["sA0"], "g", True)
                        for mh in range(8):
                            m = dh2 * 8 + mh
                            lastW = None
                            for kc in range(NF):
                                lastW = p.emit("tensor", lambda e, kc=kc, m=m: e.matmul(
                                    ps_W[:, :], wout_sb[kc][:, m * 128:(m + 1) * 128],
                                    att[:, kc * 512:(kc + 1) * 512],
                                    start=(kc == 0), stop=(kc == NF - 1)),
                                    waits=([t_wout, prev_att_user, prev_wevac]
                                           if kc == 0 else []),
                                    inc="pe" if kc == NF - 1 else None)
                            ys, ysw = yrot.acquire()
                            t_ye = p.emit("vector", lambda e, ys=ys: e.tensor_copy(
                                y3_sb[ys][:, :], ps_W[:, :]), waits=[lastW] + ysw,
                                inc="dve")
                            prev_wevac = t_ye
                            for tc4 in range(4):
                                ts2, tsw = oro.acquire()
                                t_yt = p.emit("tensor", lambda e, ys=ys, ts2=ts2, tc4=tc4: e.transpose(
                                    ps_O[ts2][:, :],
                                    y3_sb[ys][:, tc4 * 128:(tc4 + 1) * 128],
                                    ident32[:, :]), waits=[t_ye] + tsw, inc="pe")
                                t_ya = p.emit("vector", lambda e, ts2=ts2, tc4=tc4, mh=mh: e.tensor_add(
                                    x1st3[:, tc4 * 1024 + mh * 128:
                                          tc4 * 1024 + (mh + 1) * 128],
                                    ps_O[ts2][:, :],
                                    xres3[:, tc4 * 1024 + mh * 128:
                                          tc4 * 1024 + (mh + 1) * 128]),
                                    waits=[t_yt, t_xr] + prev_store[0], inc="dve")
                                prev_store[0] = []
                                oro.release(ts2, t_ya)
                                yrot.release(ys, t_ya)
                                prev_x1_user[0] = t_ya
                        stt3 = []
                        for tc4 in range(4):
                            p.dma("gpsimd", "out0",
                                  x1_sp[c * 512 + tc4 * 128: c * 512 + (tc4 + 1) * 128,
                                        dh2 * 1024:(dh2 + 1) * 1024],
                                  x1st3[:, tc4 * 1024:(tc4 + 1) * 1024],
                                  waits=[prev_x1_user[0]])
                        prev_store[0] = [("out0", p.cnt["out0"], "g", True)]

                t_x1w = ("out0", p.cnt["out0"], "g", True)
                with nc.Block() as blk:
                    p.replay(blk, sems)

            ph13.close()

            NG = 2
            GT = 1024
            NTC4 = GT // 128
            ph4o = contextlib.ExitStack()
            sb4o = lambda name, shape, dt=BF16: ph4o.enter_context(nc.sbuf_tensor(name, list(shape), dt))
            with ph4o:
                hdn = sb4o("hdn", (128, NI * GT))
                for grp4 in range(NG):
                    tok0 = grp4 * GT
                    ph4 = contextlib.ExitStack()
                    sb4 = lambda name, shape, dt=BF16: ph4.enter_context(nc.sbuf_tensor(name, list(shape), dt))
                    ps4 = lambda name, shape, dt=F32: ph4.enter_context(nc.psum_tensor(name, list(shape), dt))
                    with ph4:
                        p.set_barrier(p.engine_totals() +
                                      [("out0", p.cnt.get("out0", 0), "g", True)])
                        x4 = [sb4(f"x4_{grp4}_{i}", (128, DIM), F32) for i in range(2)]
                        sq4 = sb4(f"sq4_{grp4}", (128, DIM), F32)
                        xf_tm = [sb4(f"xftm{grp4}_{i}", (128, DIM)) for i in range(2)]
                        xf = sb4(f"xf_{grp4}", (128, NF * GT))
                        gsl = [sb4(f"gsl{grp4}_{i}", (128, NF * 384)) for i in range(2)]
                        usl = [sb4(f"usl{grp4}_{i}", (128, NF * 384)) for i in range(2)]
                        sil = sb4(f"sil_{grp4}", (128, 1024), F32)
                        s4a = sb4(f"s4a_{grp4}", (128, 16), F32)
                        s4b = sb4(f"s4b_{grp4}", (128, 16), F32)
                        s4c = sb4(f"s4c_{grp4}", (128, 16), F32)
                        s4d = sb4(f"s4d_{grp4}", (128, 16), F32)
                        s4e = sb4(f"s4e_{grp4}", (128, 16), F32)
                        ps_g = [ps4(f"ps_g{grp4}_{i}", (128, 512)) for i in range(2)]
                        ps_u = [ps4(f"ps_u{grp4}_{i}", (128, 512)) for i in range(2)]
                        ps_x = [ps4(f"ps_x{grp4}_{i}", (128, 128), BF16) for i in range(2)]

                        x4rot = Rot(2)
                        for tc in range(NTC4):
                            s, wt = x4rot.acquire()
                            t_x = p.dma("sync", f"qx{s}", x4[s][:],
                                        x1_sp[tok0 + tc * 128: tok0 + (tc + 1) * 128, :],
                                        waits=wt + [t_x1w])
                            v = "vector"
                            p.emit(v, lambda e, s=s: e.tensor_mul(sq4[:], x4[s][:], x4[s][:]),
                                   waits=[t_x])
                            p.emit(v, lambda e, s=s, tc=tc: e.reduce_sum(
                                s4a[:, tc:tc + 1], x4[s][:], AX.X))
                            t_r = p.emit(v, lambda e, tc=tc: e.reduce_sum(
                                s4b[:, tc:tc + 1], sq4[:], AX.X), inc="dve")
                            x4rot.release(s, t_r)
                        v = "vector"
                        SP4 = lambda: p.emit(v, lambda e: e.tensor_copy(s4e[:, 15:16],
                                                                        s4a[:, 0:1]))
                        SP4(); SP4(); SP4()
                        p.emit(v, lambda e: e.tensor_scalar_mul(s4c[:, 0:8], s4a[:, 0:8], 1.0 / DIM))
                        p.emit(v, lambda e: e.tensor_scalar_mul(s4d[:, 0:8], s4b[:, 0:8], 1.0 / DIM))
                        SP4(); SP4()
                        p.emit(v, lambda e: e.tensor_mul(s4e[:, 0:8], s4c[:, 0:8], s4c[:, 0:8]))
                        SP4(); SP4()
                        t_var4 = p.emit(v, lambda e: e.tensor_sub(s4d[:, 0:8], s4d[:, 0:8], s4e[:, 0:8]),
                                        inc="dve")
                        t_sq4 = p.emit("scalar", lambda e: e.activation(
                            s4e[:, 0:8], s4d[:, 0:8], AF.Sqrt, bias=eps4_c[:, :]),
                            waits=[t_var4, t_ep4], inc="act")
                        p.emit(v, lambda e: e.reciprocal(s4d[:, 8:16], s4e[:, 0:8]), waits=[t_sq4])
                        SP4(); SP4(); SP4()
                        p.emit(v, lambda e: e.tensor_mul(s4c[:, 8:16], s4c[:, 0:8], s4d[:, 8:16]))
                        SP4(); SP4()
                        t_st4 = p.emit(v, lambda e: e.tensor_scalar_mul(
                            s4c[:, 8:16], s4c[:, 8:16], -1.0), inc="dve")
                        SP4(); SP4(); SP4()
                        xtra = Rot(2)
                        xf_done = None
                        xh_rd = [None, None]
                        for tc in range(NTC4):
                            s, wt = x4rot.acquire()
                            t_x = p.dma("sync", f"qx{s}", x4[s][:],
                                        x1_sp[tok0 + tc * 128: tok0 + (tc + 1) * 128, :],
                                        waits=wt)
                            hs = tc % 2
                            t_xh = p.emit(v, lambda e, s=s, hs=hs, tc=tc: e.tensor_scalar(
                                xf_tm[hs][:], x4[s][:], s4d[:, 8 + tc:9 + tc],
                                s4c[:, 8 + tc:9 + tc], ALU.mult, ALU.add),
                                waits=[t_x, t_st4, xh_rd[hs]],
                                inc="dve")
                            x4rot.release(s, t_xh)
                            for f in range(NF):
                                ts, twt = xtra.acquire()
                                t_tp = p.emit("tensor", lambda e, f=f, hs=hs, ts=ts: e.transpose(
                                    ps_x[ts][:, :], xf_tm[hs][:, f * 128:(f + 1) * 128],
                                    ident[:, :]), waits=twt + [t_xh, t_id], inc="pe")
                                if f == NF - 1:
                                    xh_rd[hs] = t_tp
                                t_ev = p.emit(v, lambda e, f=f, tc=tc, ts=ts: e.tensor_copy(
                                    xf[:, f * GT + tc * 128: f * GT + (tc + 1) * 128],
                                    ps_x[ts][:, :]), waits=[t_tp], inc="dve")
                                xtra.release(ts, t_ev)
                                if f == NF - 1:
                                    xf_done = t_ev
                        NMG = 15
                        grot = Rot(2)
                        guguard = [[], []]
                        hdn_done = None
                        for mg in range(NMG):
                            mi0 = mg * 3
                            nmi = 3 if mg < 14 else 2
                            gs4, gw4 = grot.acquire()
                            t_g = p.dma("sync", f"kc{gs4}",
                                        gsl[gs4][:].rearrange("p (k m) -> p k m", k=NF)[
                                            :, :, 0:nmi * 128],
                                        gate_bf.rearrange("(k p) i -> p k i", p=128)[
                                            :, :, mi0 * 128:(mi0 + nmi) * 128],
                                        waits=gw4 + [t_wbf])
                            t_u = p.dma("sync", f"kc{gs4}",
                                        usl[gs4][:].rearrange("p (k m) -> p k m", k=NF)[
                                            :, :, 0:nmi * 128],
                                        up_bf.rearrange("(k p) i -> p k i", p=128)[
                                            :, :, mi0 * 128:(mi0 + nmi) * 128])
                            t_gu = (f"kc{gs4}", p.cnt[f"kc{gs4}"], "g", True)
                            lastgu = None
                            for mi in range(nmi):
                                m = mi0 + mi
                                for n2 in range(2):
                                    pgs = (m * 2 + n2) % 2
                                    gw9 = guguard[pgs]
                                    guguard[pgs] = []
                                    for which, slab, pp4 in ((0, gsl, ps_g), (1, usl, ps_u)):
                                        for kc in range(NF):
                                            lastgu = p.emit("tensor", lambda e, kc=kc, mi=mi, n2=n2, which=which, gs4=gs4, pgs=pgs, slab=slab, pp4=pp4: e.matmul(
                                                pp4[pgs][:, :],
                                                slab[gs4][:, kc * 384 + mi * 128:
                                                          kc * 384 + (mi + 1) * 128],
                                                xf[:, kc * GT + n2 * 512:
                                                   kc * GT + (n2 + 1) * 512],
                                                start=(kc == 0), stop=(kc == NF - 1)),
                                                waits=([t_gu, xf_done] + gw9
                                                       if kc == 0 else []),
                                                inc="pe" if kc == NF - 1 else None)
                                    t_si = p.emit("scalar", lambda e, pgs=pgs, m=m: e.activation(
                                        sil[:, pgs * 512:(pgs + 1) * 512], ps_g[pgs][:, :],
                                        AF.Silu,
                                        scale=scales_g[:, m:m + 1]), waits=[lastgu],
                                        inc="act")
                                    t_hd = p.emit("vector", lambda e, pgs=pgs, m=m, n2=n2: e.scalar_tensor_tensor(
                                        hdn[:, m * GT + n2 * 512: m * GT + (n2 + 1) * 512],
                                        ps_u[pgs][:, :], scales_u[:, m:m + 1],
                                        sil[:, pgs * 512:(pgs + 1) * 512],
                                        ALU.mult, ALU.mult), waits=[t_si], inc="dve")
                                    guguard[pgs] = [t_si, t_hd]
                                    hdn_done = t_hd
                            grot.release(gs4, lastgu)
                        t_p4a = [("pe", p.cnt["pe"], "n", True),
                                 ("dve", p.cnt["dve"], "n", True),
                                 ("act", p.cnt["act"], "n", True)]
                        with nc.Block() as blk:
                            p.replay(blk, sems)
                    ph4b = contextlib.ExitStack()
                    sb4b = lambda name, shape, dt=BF16: ph4b.enter_context(nc.sbuf_tensor(name, list(shape), dt))
                    ps4b = lambda name, shape, dt=F32: ph4b.enter_context(nc.psum_tensor(name, list(shape), dt))
                    with ph4b:
                        p.set_barrier(p.engine_totals() +
                                      [("out0", p.cnt.get("out0", 0), "g", True)])
                        dsl = [sb4b(f"dsl{grp4}_{i}", (128, NI * 128)) for i in range(2)]
                        y4 = [sb4b(f"y4_{grp4}_{i}", (128, 1024), F32) for i in range(2)]
                        xr4 = sb4b(f"xr4_{grp4}", (128, NTC4 * 512), F32)
                        o4 = sb4b(f"o4_{grp4}", (128, NTC4 * 512), F32)
                        ps_d = [ps4b(f"ps_d{grp4}_{i}", (128, 512)) for i in range(2)]
                        ps_y = [ps4b(f"ps_y{grp4}_{i}", (128, 128)) for i in range(2)]
                        drot = Rot(2)
                        y4r = Rot(2)
                        y4t = Rot(2)
                        dguard = [[], []]
                        pend4 = None
                        prev_o4 = [t_p4a[1], t_x1w]
                        prev_ost = []

                        def emit_tail4(it):
                            nonlocal prev_o4, prev_ost
                            ys4, m, mh4, t_ye4, _txr = it
                            for tc8 in range(8):
                                ts4, tsw4 = y4t.acquire()
                                t_yt4 = p.emit("tensor", lambda e, ys4=ys4, ts4=ts4, tc8=tc8: e.transpose(
                                    ps_y[ts4][:, :],
                                    y4[ys4][:, (tc8 % 4) * 128:(tc8 % 4 + 1) * 128]
                                    if False else
                                    y4[ys4][:, tc8 * 128:(tc8 + 1) * 128],
                                    ident32[:, :]), waits=[t_ye4] + tsw4, inc="pe")
                                t_ya4 = p.emit("vector", lambda e, ts4=ts4, tc8=tc8, mh4=mh4: e.tensor_add(
                                    o4[:, tc8 * 512 + mh4 * 128:
                                       tc8 * 512 + (mh4 + 1) * 128],
                                    ps_y[ts4][:, :],
                                    xr4[:, tc8 * 512 + mh4 * 128:
                                        tc8 * 512 + (mh4 + 1) * 128]),
                                    waits=[t_yt4, _txr] + prev_ost, inc="dve")
                                prev_ost = []
                                y4t.release(ts4, t_ya4)
                                y4r.release(ys4, t_ya4)
                                prev_o4 = [t_ya4]
                        for mq in range(4):
                            t_xr4 = None
                            for tc4 in range(NTC4):
                                t_xr4 = p.dma("gpsimd", "sA1",
                                              xr4[:, tc4 * 512:(tc4 + 1) * 512],
                                              x1_sp[tok0 + tc4 * 128: tok0 + (tc4 + 1) * 128,
                                                    mq * 512:(mq + 1) * 512],
                                              waits=(prev_o4 if tc4 == 0 else []))
                            t_xr4 = ("sA1", p.cnt["sA1"], "g", True)
                            prev_o4 = []
                            for mh4 in range(4):
                                m = mq * 4 + mh4
                                ds4, dw4 = drot.acquire()
                                p.dma("sync", f"kx{ds4}",
                                      dsl[ds4][:].rearrange("p (k m) -> p k m", k=NI),
                                      down_bf.rearrange("(k p) d -> p k d", p=128)[
                                          :, :, m * 128:(m + 1) * 128],
                                      waits=dw4 + [t_wbf])
                                t_ds = (f"kx{ds4}", p.cnt[f"kx{ds4}"], "g", True)
                                ys4, ysw4 = y4r.acquire()
                                t_last_ev = None
                                for n2 in range(2):
                                    pds = n2
                                    dw9 = dguard[pds]
                                    dguard[pds] = []
                                    lastd = None
                                    for kc in range(NI):
                                        lastd = p.emit("tensor", lambda e, kc=kc, ds4=ds4, pds=pds, n2=n2: e.matmul(
                                            ps_d[pds][:, :],
                                            dsl[ds4][:, kc * 128:(kc + 1) * 128],
                                            hdn[:, kc * GT + n2 * 512:
                                                kc * GT + (n2 + 1) * 512],
                                            start=(kc == 0), stop=(kc == NI - 1)),
                                            waits=([t_ds, hdn_done] + dw9 + (ysw4 if n2 == 0 else [])
                                                   if kc == 0 else []),
                                            inc="pe" if kc == NI - 1 else None)
                                    t_ye4 = p.emit("vector", lambda e, ys4=ys4, pds=pds, m=m, n2=n2: e.tensor_scalar_mul(
                                        y4[ys4][:, n2 * 512:(n2 + 1) * 512],
                                        ps_d[pds][:, :],
                                        scales_d[:, m:m + 1]), waits=[lastd],
                                        inc="dve")
                                    dguard[pds] = [t_ye4]
                                    t_last_ev = t_ye4
                                if pend4 is not None:
                                    emit_tail4(pend4)
                                pend4 = (ys4, m, mh4, t_last_ev, t_xr4)
                                drot.release(ds4, lastd)
                            if pend4 is not None:
                                emit_tail4(pend4)
                                pend4 = None
                            for tc4 in range(NTC4):
                                p.dma("gpsimd", "out1",
                                      out_h[tok0 + tc4 * 128: tok0 + (tc4 + 1) * 128,
                                            mq * 512:(mq + 1) * 512],
                                      o4[:, tc4 * 512:(tc4 + 1) * 512],
                                      waits=[prev_o4[0]])
                            prev_ost = [("out1", p.cnt["out1"], "g", True)]
                        with nc.Block() as blk:
                            p.replay(blk, sems)

            def final_wait(e):
                for s, v in p.cnt.items():
                    e.wait_ge(sems[s], v)
            with nc.Block() as blk:
                blk.gpsimd(final_wait)

    return nc


def _prep_shared(inp):
    f32 = np.float32
    d = {}
    d["wqkv"] = np.ascontiguousarray(inp["L0_attn_w_qkv"], f32)
    d["rmsw_c"] = np.ascontiguousarray(inp["L0_attn_rms_w"].reshape(NF, 128).T, f32)
    d["wcomb"] = np.ascontiguousarray(inp["L0_attn_w_comb"], f32)
    d["bcomb"] = np.ascontiguousarray(inp["L0_attn_b_comb"].reshape(1, 96), f32)
    kc, vc = inp["L0_attn_kc_w"], inp["L0_attn_vc_w"]

    def pair_itow(w):
        ev = w[0::2].transpose(0, 2, 3, 1)
        od = w[1::2].transpose(0, 2, 3, 1)
        return np.concatenate([ev, od], axis=1)
    d["kcw_l"] = np.ascontiguousarray(
        np.stack([pair_itow(kc), pair_itow(vc)], axis=1), f32)
    kp, vp = inp["L0_attn_k_pos"], inp["L0_attn_v_pos"]

    def pair_pos(w):
        ev = w[0::2].transpose(0, 2, 1)
        od = w[1::2].transpose(0, 2, 1)
        return np.concatenate([ev, od], axis=1)
    d["kpos_l"] = np.ascontiguousarray(
        np.stack([pair_pos(kp), pair_pos(vp)], axis=1), f32)
    kb, vb = inp["L0_attn_kc_b"], inp["L0_attn_vc_b"]

    def pair_b(w):
        return np.concatenate([w[0::2], w[1::2]], axis=1).T
    d["kcb_l"] = np.ascontiguousarray(
        np.concatenate([pair_b(kb), pair_b(vb)], axis=1), f32)
    mm = inp["L0_attn_mem_kv"]

    def pair_m(w):
        ev = w[0::2].transpose(0, 2, 1)
        od = w[1::2].transpose(0, 2, 1)
        return np.concatenate([ev, od], axis=1)
    d["mem_l"] = np.ascontiguousarray(np.stack([pair_m(mm[0]), pair_m(mm[1])]), f32)
    d["wout"] = np.ascontiguousarray(inp["L0_attn_w_out"], f32)
    d["gateT"] = np.ascontiguousarray(inp["L0_ffn_gate_w"].T, f32)
    d["upT"] = np.ascontiguousarray(inp["L0_ffn_up_w"].T, f32)
    d["downT"] = np.ascontiguousarray(inp["L0_ffn_down_w"].T, f32)
    d["gs_c"] = np.ascontiguousarray(inp["L0_ffn_gate_s"].reshape(NI, 128).T, f32)
    d["us_c"] = np.ascontiguousarray(inp["L0_ffn_up_s"].reshape(NI, 128).T, f32)
    d["ds_c"] = np.ascontiguousarray(inp["L0_ffn_down_s"].reshape(NF, 128).T, f32)
    d["ident_h"] = np.eye(128, dtype=BF)
    d["ident32_h"] = np.eye(128, dtype=np.float32)
    return d


def _masks(h):
    i = np.arange(128)[:, None]
    j = np.arange(NKK)[None, :]
    std = (i <= j) & (j <= i + 128)
    first = std & ((j >= 128) if h == 0 else True)
    lm = np.full((2, 128, NKK), NEG, np.float32)
    lm[0][first] = 0.0
    lm[1][std] = 0.0
    cm = np.full((16, 128, NCK), NEG, np.float32)
    for tc in range(16):
        ig = h * T + tc * 128 + np.arange(128)[:, None]
        g = np.arange(NCK)[None, :] - 4
        ok = (g >= 0) & (g >= ig // CBS)
        cm[tc][np.broadcast_to(ok, (128, NCK))] = 0.0
    lm2 = np.concatenate([lm, lm], axis=2)
    cm2 = np.concatenate([cm, cm], axis=2)
    return lm2.astype(BF), cm2.astype(BF)


def make_in_maps(inp):
    shared = _prep_shared(inp)
    x = np.asarray(inp["L0_attn_inp"], np.float32)
    maps = []
    for c in range(8):
        b, h = c // 2, c % 2
        xs = np.zeros((RN, DIM), np.float32)
        xs[E:] = x[b, h * T:(h + 1) * T]
        if h == 1:
            xs[:E] = x[b, T - E:T]
        lm, cm = _masks(h)
        m = dict(shared)
        m["x_in"] = xs
        m["lmask_h"] = lm
        m["cmask_h"] = cm
        maps.append(m)
    return maps


def kernel(**inp):
    nc = build_nc(debug=0)
    maps = make_in_maps(inp)
    res = run_bass_kernel_spmd(nc, maps, core_ids=list(range(8)))
    out = np.empty((4, 4096, DIM), np.float32)
    for c in range(8):
        b, h = c // 2, c % 2
        out[b, h * T:(h + 1) * T] = res.results[c]["out"]
    return out


# version 14
# speedup vs baseline: 1.0526x; 1.0333x over previous
import numpy as np
import ml_dtypes
import concourse.bass as bass
import concourse.mybir as mybir
from concourse.bass_utils import run_bass_kernel_spmd

F32 = mybir.dt.float32
BF16 = mybir.dt.bfloat16
AF = mybir.ActivationFunctionType
ALU = mybir.AluOpType
AX = mybir.AxisListType
BF = ml_dtypes.bfloat16

H, DH, DIM = 32, 64, 2048
T, E = 2048, 128
RN = T + E
NF = DIM // 128
NP = H // 2
SCALE = DH ** -0.5
NEG = -1e30
EPS_LN = 1e-5
EPS_RMS = float(np.finfo(np.float32).eps)
INTER = 5632
NI = INTER // 128
CBS = 64
NBLK = T // CBS
NCK = 68
NKK = 256


ENGINES = ("sync", "tensor", "vector", "scalar", "gpsimd")


class Prog:

    def __init__(self):
        self.ops = {e: [] for e in ENGINES}
        self.cnt = {}
        self.pending = {}

    def set_barrier(self, tokens):
        for e in ENGINES:
            self.pending[e] = list(tokens)

    def engine_totals(self):
        return [(s, v, "n", True) for s, v in self.cnt.items()
                if s in ("pe", "dve", "act", "pool")]

    def emit(self, eng, fn, waits=(), inc=None, amt=1):
        wl = []
        waits = list(waits) + self.pending.pop(eng, [])
        for w in waits:
            if w is None:
                continue
            sem, val, peng, asy = w
            if peng == eng and not asy:
                continue
            wl.append((sem, val))
        tok = None
        if inc is not None:
            self.cnt[inc] = self.cnt.get(inc, 0) + amt
            tok = (inc, self.cnt[inc], eng, eng in ("sync", "gpsimd") and amt % 16 == 0)
        self.ops[eng].append((fn, wl, inc, amt))
        return tok

    def dma(self, queue, sem, out, in_, waits=(), **kw):
        return self.emit(
            queue,
            lambda e: e.dma_start(out=out, in_=in_, **kw),
            waits=waits, inc=sem, amt=16)

    def replay(self, block, sems):
        for eng in ENGINES:
            section = self.ops[eng]
            if not section:
                continue

            def run(engine, section=section):
                seen = {}
                for fn, waits, inc, amt in section:
                    for s, v in waits:
                        if seen.get(s, -1) >= v:
                            continue
                        engine.wait_ge(sems[s], v)
                        seen[s] = v
                    ins = fn(engine)
                    if inc is not None:
                        ins.then_inc(sems[inc], amt)
            getattr(block, eng)(run)
        self.ops = {e: [] for e in ENGINES}


class Rot:

    def __init__(self, n):
        self.n = n
        self.tok = [[] for _ in range(n)]
        self.i = 0

    def acquire(self):
        s = self.i % self.n
        self.i += 1
        waits = self.tok[s]
        self.tok[s] = []
        return s, waits

    def release(self, s, tok):
        if tok is not None:
            self.tok[s].append(tok)


def build_nc(debug=0):
    nc = bass.Bass("TRN2", target_bir_lowering=False,
                   detect_race_conditions=False)
    p = Prog()

    def din(name, shape, dtype=F32):
        return nc.dram_tensor(name, list(shape), dtype, kind="ExternalInput").ap()

    x_in = din("x_in", (RN, DIM))
    wqkv = din("wqkv", (DIM, 3 * DIM))
    rmsw_c = din("rmsw_c", (128, NF))
    wcomb = din("wcomb", (DIM, 96))
    bcomb = din("bcomb", (1, 96))
    kcw_l = din("kcw_l", (NP, 2, 128, CBS, 64))
    kpos_l = din("kpos_l", (NP, 2, 128, CBS))
    kcb_l = din("kcb_l", (128, 2 * NP))
    mem_l = din("mem_l", (2, NP, 128, 4))
    wout = din("wout", (DIM, DIM))
    gateT = din("gateT", (DIM, INTER))
    upT = din("upT", (DIM, INTER))
    downT = din("downT", (INTER, DIM))
    gate_bf = nc.dram_tensor("gate_bf", [DIM, INTER], BF16, kind="Internal").ap()
    up_bf = nc.dram_tensor("up_bf", [DIM, INTER], BF16, kind="Internal").ap()
    down_bf = nc.dram_tensor("down_bf", [INTER, DIM], BF16, kind="Internal").ap()
    gs_c = din("gs_c", (128, NI))
    us_c = din("us_c", (128, NI))
    ds_c = din("ds_c", (128, NF))
    ident_h = din("ident_h", (128, 128), BF16)
    ident32_h = din("ident32_h", (128, 128), F32)
    lmask_h = din("lmask_h", (2, 128, 2 * NKK), BF16)
    cmask_h = din("cmask_h", (16, 128, 2 * NCK), BF16)

    out_h = nc.dram_tensor("out", [T, DIM], F32, kind="ExternalOutput").ap()

    kind_sp = "ExternalOutput" if debug else "Internal"
    q_sp = nc.dram_tensor("q_sp", [NF, 128, T], BF16, kind=kind_sp).ap()
    k_sp = nc.dram_tensor("k_sp", [NF, 128, RN], BF16, kind=kind_sp).ap()
    v_sp = nc.dram_tensor("v_sp", [NF + 1, 128, DIM], BF16, kind=kind_sp).ap()
    ck_own = nc.dram_tensor("ck_own", [2, NP, 128, NBLK], F32, kind="Internal").ap()
    ck_all = nc.dram_tensor("ck_all", [2, 2, NP, 128, NBLK], F32, kind="Internal").ap()
    x1_sp = nc.dram_tensor("x1_sp", [T, DIM], F32,
                           kind="ExternalOutput" if debug else "Internal").ap()
    if debug:
        xha_dbg = nc.dram_tensor("xha_dbg", [128, NF * RN], BF16, kind="ExternalOutput").ap()
        cko_dbg = nc.dram_tensor("cko_dbg", [2, NP, 128, NBLK], F32, kind="ExternalOutput").ap()
        st_dbg = nc.dram_tensor("st_dbg", [128, 64], F32, kind="ExternalOutput").ap()
        cw_dbg = nc.dram_tensor("cw_dbg", [128, 16 * 96], F32, kind="ExternalOutput").ap()
        ckt_dbg = nc.dram_tensor("ckt_dbg", [2, 128, NP * 2 * NCK], BF16, kind="ExternalOutput").ap()

    sem_names = ["qc", "qw", "qsl", "pe", "dve", "act", "cc",
                 "qx0", "qx1", "kx0", "kx1", "kc0", "kc1", "kw0", "kw1",
                 "qw0", "qw1", "cko", "qld", "qdb", "sA0", "sA1", "sB0", "sB1",
                 "sC0", "sC1", "sD0", "sD1", "out0", "out1", "pool"]

    import contextlib
    ctx = contextlib.ExitStack()
    with ctx:
        sems = {s: ctx.enter_context(nc.semaphore(s)) for s in sem_names}

        pers = contextlib.ExitStack()
        sb = lambda name, shape, dt=BF16: pers.enter_context(nc.sbuf_tensor(name, list(shape), dt))
        with pers:
            ident = sb("ident", (128, 128))
            ident32 = sb("ident32", (128, 128), F32)
            rmsw_s = sb("rmsw_s", (128, NF), F32)
            scales_g = sb("scales_g", (128, NI), F32)
            scales_u = sb("scales_u", (128, NI), F32)
            scales_d = sb("scales_d", (128, NF), F32)
            eps4_c = sb("eps4_c", (128, 1), F32)
            ph13 = contextlib.ExitStack()
            sb13 = lambda name, shape, dt=BF16: ph13.enter_context(nc.sbuf_tensor(name, list(shape), dt))
            cw_tm = sb13("cw_tm", (128, 16 * 96), F32)
            ck_sb = sb13("ck_sb", (128, NP * 2 * NCK))
            cvt_sb = sb13("cvt_sb", (128, H * 64))
            lmask = sb13("lmask", (128, 2 * 2 * NKK))
            cmask = sb13("cmask", (128, 16 * 2 * NCK))

            t_ep4 = p.emit("vector", lambda e: e.memset(eps4_c[:], EPS_LN), inc="dve")
            p.dma("sync", "qc", ident[:], ident_h[:, :])
            p.dma("sync", "qc", ident32[:], ident32_h[:, :])
            p.dma("sync", "qc", rmsw_s[:], rmsw_c[:, :])
            p.dma("sync", "qc", lmask[:].rearrange("p (s n) -> p s n", s=2),
                  lmask_h.rearrange("s p n -> p s n"))
            p.dma("sync", "qc", cmask[:].rearrange("p (s n) -> p s n", s=16),
                  cmask_h.rearrange("s p n -> p s n"))
            p.dma("sync", "qc", scales_g[:], gs_c[:, :])
            p.dma("sync", "qc", scales_u[:], us_c[:, :])
            p.dma("sync", "qc", scales_d[:], ds_c[:, :])

            ph1 = contextlib.ExitStack()
            sb1 = lambda name, shape, dt=BF16: ph1.enter_context(nc.sbuf_tensor(name, list(shape), dt))
            ps1 = lambda name, shape, dt=F32: ph1.enter_context(nc.psum_tensor(name, list(shape), dt))
            with ph1:
                xha = sb1("xha", (128, NF * RN))
                xt = [sb1(f"xt{i}", (128, DIM), F32) for i in range(2)]
                sqt = sb1("sqt", (128, DIM), F32)
                xhtm = [sb1(f"xhtm{i}", (128, DIM)) for i in range(2)]
                cols = sb1("cols", (128, 16), F32)
                ssum = sb1("ssum", (128, NF + 1), F32)
                ssq = sb1("ssq", (128, NF + 1), F32)
                stm = sb1("stm", (128, NF + 1), F32)
                stq = sb1("stq", (128, NF + 1), F32)
                stt = sb1("stt", (128, NF + 1), F32)
                stv = sb1("stv", (128, NF + 1), F32)
                str1 = sb1("str1", (128, NF + 1), F32)
                stS = sb1("stS", (128, NF + 1), F32)
                stnm = sb1("stnm", (128, NF + 1), F32)
                wslab = [sb1(f"wslab{i}", (128, DIM // 2)) for i in range(NF)]
                kx = [sb1(f"kx{i}", (128, RN + CBS)) for i in range(2)]
                kcp = [sb1(f"kcp{i}", (128, CBS * 64)) for i in range(2)]
                qstage = [sb1(f"qstage{i}", (128, 512)) for i in range(2)]
                ckst = [sb1(f"ckst{i}", (128, NBLK), F32) for i in range(2)]
                kcbs = sb1("kcbs", (128, 2 * NP), F32)
                bcomb_bc = sb1("bcomb_bc", (128, 96), F32)
                bcomb_s = sb1("bcomb_s", (1, 96))
                cwtmp = sb1("cwtmp", (128, 96), F32)
                stcap = sb1("stcap", (128, 64), F32) if debug else None

                epsln_c = sb1("epsln_c", (128, 1), F32)
                epsrm_c = sb1("epsrm_c", (128, 1), F32)
                vtst = sb1("vtst", (128, (NF + 1) * 128))
                pst = [ps1(f"pst{i}", (128, 128), BF16) for i in range(2)]
                ps_vt = ps1("ps_vt", (128, 1024), BF16)
                psb = [ps1(f"psb{i}", (128, 512)) for i in range(2)]
                psc0 = ps1("psc0", (64, NBLK + 1))
                psc1 = ps1("psc1", (128, NBLK + 1))
                pscb = ps1("pscb", (128, 96))

                p.dma("sync", "qc", kcbs[:], kcb_l[:, :])
                t_qc = ("qc", p.cnt["qc"], "sync", True)
                t_id = t_rw = t_kcb = t_qc
                t_ep1 = p.emit("vector", lambda e: e.memset(epsln_c[:], EPS_LN), inc="dve")
                t_ep2 = p.emit("vector", lambda e: e.memset(epsrm_c[:], EPS_RMS), inc="dve")
                p.dma("gpsimd", "qw", bcomb_s[:], bcomb[:, :])

                xrot, trot = Rot(2), Rot(2)
                NTC = NF + 1
                for tc in range(NTC):
                    s, wt = xrot.acquire()
                    t_x = p.dma("sync", f"qx{s}", xt[s][:],
                                x_in[tc * 128:(tc + 1) * 128, :], waits=wt)
                    v = "vector"
                    p.emit(v, lambda e, s=s: e.tensor_mul(sqt[:], xt[s][:], xt[s][:]),
                           waits=[t_x])
                    p.emit(v, lambda e, s=s, tc=tc: e.reduce_sum(
                        ssum[:, tc:tc + 1], xt[s][:], AX.X))
                    t_r = p.emit(v, lambda e, tc=tc: e.reduce_sum(
                        ssq[:, tc:tc + 1], sqt[:], AX.X), inc="dve")
                    xrot.release(s, t_r)
                v = "vector"
                SP = lambda: p.emit(v, lambda e: e.tensor_copy(cols[:, 15:16],
                                                               ssum[:, 0:1]))
                SP(); SP(); SP()
                p.emit(v, lambda e: e.tensor_scalar_mul(stm[:], ssum[:], 1.0 / DIM))
                p.emit(v, lambda e: e.tensor_scalar_mul(stq[:], ssq[:], 1.0 / DIM))
                SP(); SP()
                p.emit(v, lambda e: e.tensor_mul(stt[:], stm[:], stm[:]))
                SP(); SP()
                t_var = p.emit(v, lambda e: e.tensor_sub(stv[:], stq[:], stt[:]),
                               inc="dve")
                t_s1 = p.emit("scalar", lambda e: e.activation(
                    stt[:], stv[:], AF.Sqrt, bias=epsln_c[:, :]),
                    waits=[t_var, t_ep1], inc="act")
                p.emit(v, lambda e: e.reciprocal(str1[:], stt[:]), waits=[t_s1])
                SP(); SP(); SP()
                p.emit(v, lambda e: e.tensor_mul(stt[:], stv[:], str1[:]))
                SP(); SP()
                t_b2 = p.emit(v, lambda e: e.tensor_mul(stq[:], stt[:], str1[:]),
                              inc="dve")
                t_s2 = p.emit("scalar", lambda e: e.activation(
                    stt[:], stq[:], AF.Sqrt, bias=epsrm_c[:, :]),
                    waits=[t_b2, t_ep2], inc="act")
                p.emit(v, lambda e: e.reciprocal(stq[:], stt[:]), waits=[t_s2])
                SP(); SP(); SP()
                p.emit(v, lambda e: e.tensor_mul(stS[:], str1[:], stq[:]))
                SP(); SP()
                p.emit(v, lambda e: e.tensor_mul(stnm[:], stm[:], stS[:]))
                SP(); SP()
                t_stats = p.emit(v, lambda e: e.tensor_scalar_mul(stnm[:], stnm[:], -1.0),
                                 inc="dve")
                SP(); SP(); SP()
                if debug:
                    p.emit(v, lambda e: e.tensor_copy(stcap[:, 0:NTC], stm[:]))
                    p.emit(v, lambda e: e.tensor_copy(stcap[:, 32:32 + NTC], stS[:]))
                xa_done = []
                xh_reader = [None, None]
                for tc in range(NTC):
                    s, wt = xrot.acquire()
                    t_x = p.dma("sync", f"qx{s}", xt[s][:],
                                x_in[tc * 128:(tc + 1) * 128, :], waits=wt)
                    hs = tc % 2
                    t_xh = p.emit(v, lambda e, s=s, hs=hs, tc=tc: e.tensor_scalar(
                        xhtm[hs][:], xt[s][:], stS[:, tc:tc + 1], stnm[:, tc:tc + 1],
                        ALU.mult, ALU.add),
                        waits=[t_x, t_stats, xh_reader[hs]], inc="dve")
                    xrot.release(s, t_xh)
                    for f in range(NF):
                        ts, twt = trot.acquire()
                        t_tp = p.emit("tensor", lambda e, f=f, hs=hs, ts=ts: e.transpose(
                            pst[ts][:, :], xhtm[hs][:, f * 128:(f + 1) * 128],
                            ident[:, :]), waits=twt + [t_xh, t_id], inc="pe")
                        if f == NF - 1:
                            xh_reader[hs] = t_tp
                        t_ev = p.emit(v, lambda e, f=f, tc=tc, ts=ts: e.tensor_scalar_mul(
                            xha[:, f * RN + tc * 128: f * RN + (tc + 1) * 128],
                            pst[ts][:, :], rmsw_s[:, f:f + 1]),
                            waits=[t_tp, t_rw], inc="dve")
                        trot.release(ts, t_ev)
                        if f == NF - 1:
                            xa_done.append(t_ev)
                xa_all = xa_done[-1]

                if debug:
                    p.dma("gpsimd", "qdb", xha_dbg[:, :], xha[:], waits=[xa_all])
                    p.dma("gpsimd", "qdb", st_dbg[:, :], stcap[:], waits=[xa_all])

                brot = Rot(2)
                qsrot = Rot(2)
                kxrot = Rot(2)
                kcrot = Rot(2)
                ckrot = Rot(2)
                slab_tok = {}
                psc_guard = [[], []]

                def qkv_group(grp, prev_pe):
                    is_q = (grp == 0)
                    coff = E if is_q else 0
                    nch = [(i * 512, 512) for i in range(4)] + ([] if is_q else [(2048, 128)])
                    spill = [q_sp, k_sp, v_sp][grp]
                    last_pe = prev_pe
                    qtr_pe = [prev_pe, prev_pe]
                    for jh in range(4):
                      jo = jh * (NF // 4)
                      qsl_s = jh % 2
                      for kc in range(NF):
                        p.dma(
                            "gpsimd", "qsl",
                            wslab[kc][:, qsl_s * 512:(qsl_s + 1) * 512],
                            wqkv[kc * 128:(kc + 1) * 128,
                                 grp * DIM + jh * (DIM // 4):grp * DIM + (jh + 1) * (DIM // 4)],
                            waits=[qtr_pe[qsl_s]] if kc == 0 else [])
                      t_slabs = ("qsl", p.cnt["qsl"], "gpsimd", True)
                      for jr in range(NF // 4):
                        j = jo + jr
                        if not is_q:
                            ks, kwt = kcrot.acquire()
                            t_kc = p.dma("gpsimd", f"kc{ks}", kcp[ks][:].rearrange(
                                "p (t o) -> p t o", t=CBS),
                                kcw_l[j, grp - 1, :, :, :], waits=kwt)
                            xs, xwt = kxrot.acquire()
                            t_kp = p.dma("gpsimd", f"kx{xs}", kx[xs][:, RN:RN + CBS],
                                         kpos_l[j, grp - 1, :, :], waits=xwt)
                        ev_toks = []
                        for (c0, cw) in nch:
                            bs, bwt = brot.acquire()
                            mm = None
                            for kc in range(NF):
                                wts = [t_slabs, xa_all] + (bwt if kc == 0 else [])
                                tk = p.emit("tensor", lambda e, kc=kc, jr=jr, bs=bs, c0=c0, cw=cw, qsl_s=qsl_s: e.matmul(
                                    psb[bs][:, 0:cw],
                                    wslab[kc][:, qsl_s * 512 + jr * 128:
                                               qsl_s * 512 + (jr + 1) * 128],
                                    xha[:, kc * RN + coff + c0: kc * RN + coff + c0 + cw],
                                    start=(kc == 0), stop=(kc == NF - 1)),
                                    waits=wts, inc="pe" if kc == NF - 1 else None)
                                if tk is not None:
                                    mm = tk
                            last_pe = mm
                            if is_q:
                                ss, swt = qsrot.acquire()
                                t_ev = p.emit("vector", lambda e, bs=bs, ss=ss, cw=cw: e.tensor_scalar_mul(
                                    qstage[ss][:, 0:cw], psb[bs][:, 0:cw], SCALE),
                                    waits=[mm] + swt, inc="dve")
                                t_st = p.dma("gpsimd", f"qw{ss}", spill[j, :, c0:c0 + cw],
                                             qstage[ss][:, 0:cw], waits=[t_ev])
                                qsrot.release(ss, t_st)
                            else:
                                t_ev = p.emit("vector", lambda e, bs=bs, xs=xs, c0=c0, cw=cw: e.tensor_copy(
                                    kx[xs][:, c0:c0 + cw], psb[bs][:, 0:cw]),
                                    waits=[mm] + (xwt if c0 == 0 else []), inc="dve")
                                ev_toks.append(t_ev)
                            brot.release(bs, t_ev)
                        if is_q:
                            continue
                        if grp == 1:
                            t_sp = p.dma("sync", f"kw{xs}", spill[j, :, 0:RN],
                                         kx[xs][:, 0:RN], waits=ev_toks)
                        else:
                            cpy = None
                            for tcx in range(NF + 1):
                                bidx = tcx % 8
                                wt2 = (ev_toks if tcx == 0 else [])
                                if bidx == 0:
                                    wt2 = wt2 + [cpy]
                                t_tp = p.emit("tensor", lambda e, tcx=tcx, bidx=bidx, xs=xs: e.transpose(
                                    ps_vt[:, bidx * 128:(bidx + 1) * 128],
                                    kx[xs][:, tcx * 128:(tcx + 1) * 128], ident[:, :]),
                                    waits=wt2 + [t_id],
                                    inc="pe" if (bidx == 7 or tcx == NF) else None)
                                if bidx == 7 or tcx == NF:
                                    nb = bidx + 1
                                    cpy = p.emit("vector", lambda e, tcx=tcx, nb=nb: e.tensor_copy(
                                        vtst[:, (tcx - nb + 1) * 128:(tcx + 1) * 128],
                                        ps_vt[:, 0:nb * 128]), waits=[t_tp], inc="dve")
                            t_sp = p.dma(
                                "sync", f"kw{xs}",
                                v_sp[:, :, j * 128:(j + 1) * 128].rearrange("w p f -> p w f"),
                                vtst[:].rearrange("p (w f) -> p w f", w=NF + 1),
                                waits=[cpy])
                            last_pe = t_tp
                        lastmm = None
                        for t in range(CBS):
                            st_, sp_ = (t == 0), (t == CBS - 1)
                            wts = ([t_kc, t_kp] + ev_toks + psc_guard[0] + psc_guard[1]) \
                                if t == 0 else []
                            p.emit("tensor", lambda e, t=t, ks=ks, xs=xs, st_=st_, sp_=sp_: e.matmul(
                                psc0[:, :], kcp[ks][0:64, t * 64:(t + 1) * 64],
                                kx[xs][0:64, E + t: E + t + 64 * NBLK + 1: 64],
                                start=st_, stop=sp_), waits=wts)
                            lastmm = p.emit("tensor", lambda e, t=t, ks=ks, xs=xs, st_=st_, sp_=sp_: e.matmul(
                                psc1[64:128, :], kcp[ks][64:128, t * 64:(t + 1) * 64],
                                kx[xs][64:128, E + t: E + t + 64 * NBLK + 1: 64],
                                start=st_, stop=sp_, tile_position=(64, 64)),
                                inc="pe" if sp_ else None)
                        last_pe = lastmm
                        kcrot.release(ks, lastmm)
                        cs, cwt = ckrot.acquire()
                        kcol = (grp - 1) * NP + j
                        t_e0 = p.emit("vector", lambda e, cs=cs, kcol=kcol: e.tensor_scalar(
                            ckst[cs][0:64, :], psc0[0:64, 0:NBLK], psc0[0:64, NBLK:NBLK + 1],
                            kcbs[0:64, kcol:kcol + 1], ALU.add, ALU.add),
                            waits=[lastmm, t_kcb] + cwt, inc="dve")
                        t_e1 = p.emit("vector", lambda e, cs=cs, kcol=kcol: e.tensor_scalar(
                            ckst[cs][64:128, :], psc1[64:128, 0:NBLK], psc1[64:128, NBLK:NBLK + 1],
                            kcbs[64:128, kcol:kcol + 1], ALU.add, ALU.add), inc="dve")
                        psc_guard[0] = [t_e0]
                        psc_guard[1] = [t_e1]
                        t_cko = p.dma("gpsimd", f"sA{cs}", ck_own[grp - 1, j, :, :], ckst[cs][:],
                                      waits=[t_e0, t_e1])
                        ckrot.release(cs, t_cko)
                        kxrot.release(xs, t_sp)
                        kxrot.release(xs, last_pe)
                    return last_pe

                g_pe = qkv_group(0, None)

                wcomb_sb = sb1("wcomb_sb", (128, NF * 96))
                p.dma("gpsimd", "qw", wcomb_sb[:].rearrange("p (k c) -> p k c", k=NF),
                      wcomb.rearrange("(k p) c -> p k c", p=128))
                t_wc = t_bc = ("qw", p.cnt["qw"], "gpsimd", True)
                ones_sb = sb1("ones_sb", (1, 128))
                t_on = p.emit("vector", lambda e: e.memset(ones_sb[:], 1.0), inc="dve")
                t_bb = p.emit("tensor", lambda e: e.matmul(
                    pscb[:, :], ones_sb[:, :], bcomb_s[:, :], start=True, stop=True),
                    waits=[t_bc, t_on], inc="pe")
                t_bbe = p.emit("vector", lambda e: e.tensor_copy(bcomb_bc[:], pscb[:]),
                               waits=[t_bb], inc="dve")
                prev_ad = t_bbe
                prev_sig = None
                for tc in range(16):
                    mm = None
                    for kc in range(NF):
                        tk = p.emit("tensor", lambda e, tc=tc, kc=kc: e.matmul(
                            pscb[:, :],
                            xha[:, kc * RN + E + tc * 128: kc * RN + E + (tc + 1) * 128],
                            wcomb_sb[:, kc * 96:(kc + 1) * 96],
                            start=(kc == 0), stop=(kc == NF - 1)),
                            waits=[prev_ad, t_wc, xa_all] if kc == 0 else [],
                            inc="pe" if kc == NF - 1 else None)
                        if tk is not None:
                            mm = tk
                    t_ad = p.emit("vector", lambda e, tc=tc: e.tensor_add(
                        cwtmp[:], pscb[:], bcomb_bc[:]), waits=[mm, prev_sig], inc="dve")
                    prev_ad = t_ad
                    prev_sig = p.emit("scalar", lambda e, tc=tc: e.activation(
                        cw_tm[:, tc * 96:(tc + 1) * 96], cwtmp[:], AF.Sigmoid),
                        waits=[t_ad], inc="act")

                g_pe = qkv_group(1, g_pe)
                g_pe = qkv_group(2, g_pe)

                all_ck = [("sA0", p.cnt.get("sA0", 0), "gpsimd", True),
                          ("sA1", p.cnt.get("sA1", 0), "gpsimd", True)]
                t_cc = p.emit("gpsimd", lambda e: e.collective_compute(
                    "AllGather", ALU.bypass,
                    replica_groups=[[0, 1], [2, 3], [4, 5], [6, 7]],
                    ins=[ck_own.rearrange("kv g p n -> (kv g p) n")],
                    outs=[ck_all.rearrange("r kv g p n -> (r kv g p) n")]),
                    waits=all_ck, inc="cc")
                t_ccw = ("cc", p.cnt["cc"], "gpsimd", True)

                cv_fm = sb1("cv_fm", (128, NP * NCK))
                t_zk = p.emit("vector", lambda e: e.memset(ck_sb[:], 0.0), inc="dve")
                for hb in range(2):
                    ckv = ck_sb[hb * 64:(hb + 1) * 64].rearrange(
                        "p (g n) -> p g n", g=NP)[:, :, hb * NCK:hb * NCK + NCK]
                    for r in range(2):
                        p.dma("gpsimd", "qld",
                              ckv[:, :, 4 + r * NBLK: 4 + (r + 1) * NBLK],
                              ck_all[r, 0, :, hb * 64:(hb + 1) * 64, :].rearrange(
                                  "g p n -> p g n"),
                              waits=[t_ccw, t_zk])
                    p.dma("gpsimd", "qld", ckv[:, :, 0:4],
                          mem_l[0, :, hb * 64:(hb + 1) * 64, :].rearrange("g p n -> p g n"))
                for r in range(2):
                    p.dma("gpsimd", "qld",
                          cv_fm[:].rearrange("p (g n) -> p g n", g=NP)[
                              :, :, 4 + r * NBLK: 4 + (r + 1) * NBLK],
                          ck_all[r, 1].rearrange("g p n -> p g n"), waits=[t_ccw])
                p.dma("gpsimd", "qld",
                      cv_fm[:].rearrange("p (g n) -> p g n", g=NP)[:, :, 0:4],
                      mem_l[1].rearrange("g p n -> p g n"))
                t_ckl = ("qld", p.cnt["qld"], "gpsimd", True)
                t_z = p.emit("vector", lambda e: e.memset(cvt_sb[:], 0.0), inc="dve")
                for h in range(H):
                    ts, twt = trot.acquire()
                    b0 = (h % 2) * 64
                    t_tp = p.emit("tensor", lambda e, h=h, ts=ts, b0=b0: e.transpose(
                        pst[ts][0:NCK, 0:64],
                        cv_fm[b0:b0 + 64,
                              (h // 2) * NCK:(h // 2 + 1) * NCK],
                        ident[b0:b0 + 64, b0:b0 + 64]), waits=twt + [t_ckl], inc="pe")
                    t_ev = p.emit("vector", lambda e, h=h, ts=ts: e.tensor_copy(
                        cvt_sb[0:NCK, h * 64:(h + 1) * 64], pst[ts][0:NCK, 0:64]),
                        waits=[t_tp, t_z], inc="dve")
                    trot.release(ts, t_ev)

                if debug:
                    p.dma("gpsimd", "qdb", cko_dbg.rearrange("a b c d -> (a b c) d"),
                          ck_own.rearrange("a b c d -> (a b c) d"))
                    p.dma("gpsimd", "qdb", cw_dbg[:, :], cw_tm[:], waits=[("act", p.cnt["act"], "scalar", False)])
                    p.dma("gpsimd", "qdb", ckt_dbg[0, :, 0:NP * 2 * NCK], ck_sb[:])
                    p.dma("gpsimd", "qdb", ckt_dbg[1, :, 0:H * 64], cvt_sb[:])

                p1_join = [(s, p.cnt[s], "none", True) for s in p.cnt]

                with nc.Block() as blk:
                    p.replay(blk, sems)

            t_spq = [("qw0", p.cnt.get("qw0", 0), "g", True),
                     ("qw1", p.cnt.get("qw1", 0), "g", True)]
            t_spk = [("kw0", p.cnt.get("kw0", 0), "g", True),
                     ("kw1", p.cnt.get("kw1", 0), "g", True)]
            ph3 = contextlib.ExitStack()
            sb3 = lambda name, shape, dt=BF16: ph3.enter_context(nc.sbuf_tensor(name, list(shape), dt))
            ps3 = lambda name, shape, dt=F32: ph3.enter_context(nc.psum_tensor(name, list(shape), dt))
            with ph3:
                q_sb = sb3("q_sb", (128, NF * 512))
                vtm = sb3("vtm", (128, 5 * DIM))
                att = sb3("att", (128, NF * 512))
                wpad = [sb3(f"wpad{i}", (128, 2 * 640)) for i in range(2)]
                wout_sb = [sb3(f"wos{i}", (128, DIM)) for i in range(NF)]
                P_sb = [sb3(f"Pp{i}", (128, 648)) for i in range(3)]
                Pm_sb = [sb3(f"Pm{i}", (128, 648)) for i in range(3)]
                PT_sb = [sb3(f"PTp{i}", (128, 768)) for i in range(3)]
                y3_sb = [sb3(f"y3{i}", (128, 512), F32) for i in range(2)]
                xres3 = sb3("xres3", (128, 4 * 1024), F32)
                x1st3 = sb3("x1st3", (128, 4 * 1024), F32)
                lcol_ = [sb3(f"lcol{i}", (128, 16), F32) for i in range(3)]
                gcol_ = [sb3(f"gcol{i}", (128, 16), F32) for i in range(3)]
                spc3 = sb3("spc3", (128, 2), F32)

                ps_S = [ps3(f"ps_S{i}", (128, 512)) for i in range(2)]
                ps_C = [ps3(f"ps_C{i}", (128, 136)) for i in range(2)]
                ps_T = [ps3("ps_T0", (128, 768), BF16)]
                ps_O = [ps3(f"ps_O{i}", (128, 128)) for i in range(2)]
                ps_W = ps3("ps_W", (128, 512))

                p.set_barrier(p.engine_totals())
                for kc in range(NF):
                    p.dma("gpsimd", "qsl", wout_sb[kc][:],
                          wout[kc * 128:(kc + 1) * 128, :])
                t_wout = ("qsl", p.cnt["qsl"], "g", True)
                for dst, src_ in ((gate_bf, gateT), (up_bf, upT), (down_bf, downT)):
                    for hf in range(4):
                        r0, r1 = hf * (dst.shape[0] // 4), (hf + 1) * (dst.shape[0] // 4)
                        p.dma("gpsimd", "qw", dst[r0:r1, :], src_[r0:r1, :])
                t_wbf = ("qw", p.cnt["qw"], "g", True)
                t_z1 = p.emit("vector", lambda e: e.memset(wpad[0][:], 0.0), inc="dve")
                t_z2 = p.emit("vector", lambda e: e.memset(wpad[1][:], 0.0), inc="dve")
                t_z3 = None
                for i3 in range(3):
                    t_z3 = p.emit("vector", lambda e, i3=i3: e.memset(
                        PT_sb[i3][64:128, 512:768], 0.0), inc="dve")

                rotS = Rot(2)
                rotP = Rot(3)
                wpr = Rot(2)
                yrot = Rot(2)
                ytr = Rot(2)
                oro = Rot(2)
                ptr = Rot(1)
                prev_q_user = None
                prev_v_user = None
                prev_att_user = None
                prev_wevac = None
                prev_x1_user = [None, None]
                prev_store = [[], []]
                SP3 = lambda: p.emit("vector", lambda e: e.tensor_copy(spc3[:, 0:1], lcol[:, 0:1]))
                for c in range(4):
                    for fh in range(2):
                        p.dma("sync", "sB0",
                              q_sb[:, fh * 8 * 512:(fh + 1) * 8 * 512].rearrange(
                                  "p (f t) -> p f t", f=8),
                              q_sp[fh * 8:(fh + 1) * 8, :,
                                   c * 512:(c + 1) * 512].rearrange("f p t -> p f t"),
                              waits=(t_spq + [prev_q_user]) if fh == 0 else [])
                    t_q = ("sB0", p.cnt["sB0"], "g", True)
                    for (w0, w1) in ((0, 3), (3, 5)):
                        p.dma("sync", "sD0",
                              vtm[:, w0 * DIM:w1 * DIM].rearrange(
                                  "p (w f) -> p w f", w=w1 - w0),
                              v_sp[c * 4 + w0:c * 4 + w1, :, :].rearrange("w p f -> p w f"),
                              waits=(t_spk + [prev_v_user]) if w0 == 0 else [])
                    t_v = ("sD0", p.cnt["sD0"], "g", True)
                    last_pe3 = None
                    pendq = []

                    def emit_tail(it):
                        nonlocal last_pe3, prev_att_user
                        pP, g, qc, tc, t_exp = it
                        lcol = lcol_[pP]
                        gcol = gcol_[pP]
                        cwb = tc * 96 + 6 * g
                        t_r1 = p.emit("vector", lambda e, lcol=lcol: e.reciprocal(
                            lcol[:, 4:6], lcol[:, 0:2]),
                            waits=[t_exp])
                        p.emit("vector", lambda e, lcol=lcol: e.reciprocal(
                            lcol[:, 6:8], lcol[:, 2:4]))
                        p.emit("vector", lambda e, cwb=cwb, gcol=gcol: e.tensor_add(
                            gcol[:, 4:6], cw_tm[:, cwb:cwb + 4:3],
                            cw_tm[:, cwb + 1:cwb + 5:3]))
                        SP3(); SP3()
                        p.emit("vector", lambda e, gcol=gcol, lcol=lcol: e.tensor_mul(
                            gcol[:, 6:8], gcol[:, 4:6], lcol[:, 4:6]))
                        t_gt = p.emit("vector", lambda e, cwb=cwb, gcol=gcol, lcol=lcol: e.tensor_mul(
                            gcol[:, 8:10], cw_tm[:, cwb + 2:cwb + 6:3],
                            lcol[:, 6:8]), inc="dve")
                        t_poolS = None
                        for hb in range(2):
                            t_poolS = p.emit("vector", lambda e, pP=pP, hb=hb, gcol=gcol: e.tensor_scalar_mul(
                                Pm_sb[pP][:, hb * 256:(hb + 1) * 256],
                                P_sb[pP][:, hb * 256:(hb + 1) * 256],
                                gcol[:, 6 + hb:7 + hb]),
                                waits=[t_gt] if hb == 0 else [],
                                inc="dve" if hb == 1 else None)
                        t_poolC = None
                        for hb in range(2):
                            t_poolC = p.emit("vector", lambda e, pP=pP, hb=hb, gcol=gcol: e.tensor_scalar_mul(
                                Pm_sb[pP][:, 512 + hb * 68:512 + (hb + 1) * 68],
                                P_sb[pP][:, 512 + hb * 68:512 + (hb + 1) * 68],
                                gcol[:, 8 + hb:9 + hb]),
                                inc="dve" if hb == 1 else None)
                        pts, ptw = ptr.acquire()
                        tt = None
                        for k6 in range(4):
                            tt = p.emit("tensor", lambda e, k6=k6, pP=pP: e.transpose(
                                ps_T[0][:, k6 * 128:(k6 + 1) * 128],
                                Pm_sb[pP][:, k6 * 128:(k6 + 1) * 128], ident[:, :]),
                                waits=[t_poolS, t_poolC] + (ptw if k6 == 0 else []))
                        for hb in range(2):
                            tt = p.emit("tensor", lambda e, hb=hb, pP=pP: e.transpose(
                                ps_T[0][0:NCK, 512 + hb * 128:512 + hb * 128 + 128],
                                Pm_sb[pP][:, 512 + hb * 68:512 + (hb + 1) * 68],
                                ident[:, :]),
                                inc="pe" if hb == 1 else None)
                        p.emit("vector", lambda e, pP=pP: e.tensor_copy(
                            PT_sb[pP][:, 0:512], ps_T[0][:, 0:512]),
                            waits=[tt, t_z3])
                        t_ptev = p.emit("vector", lambda e, pP=pP: e.tensor_copy(
                            PT_sb[pP][0:NCK, 512:768], ps_T[0][0:NCK, 512:768]),
                            inc="dve")
                        ptr.release(pts, t_ptev)
                        so, sow = oro.acquire()
                        lastO = None
                        for hb in range(2):
                            for st3 in range(3):
                                if st3 < 2:
                                    lhs = vtm[:, (qc + st3) * DIM + (2 * g + hb) * 64:
                                              (qc + st3) * DIM + (2 * g + hb) * 64 + 64]
                                    rhs = PT_sb[pP][:, hb * 256 + st3 * 128:
                                                    hb * 256 + (st3 + 1) * 128]
                                else:
                                    lhs = cvt_sb[:, (2 * g + hb) * 64:(2 * g + hb + 1) * 64]
                                    rhs = PT_sb[pP][:, 512 + hb * 128:512 + (hb + 1) * 128]
                                lastO = p.emit("tensor", lambda e, lhs=lhs, rhs=rhs, hb=hb,
                                               st3=st3, so=so: e.matmul(
                                    ps_O[so][hb * 64:(hb + 1) * 64, :], lhs, rhs,
                                    start=(st3 == 0), stop=(st3 == 2),
                                    tile_position=(0, hb * 64)),
                                    waits=([t_ptev, t_v, t_ckl] + sow
                                           if (hb == 0 and st3 == 0) else []),
                                    inc="pe" if (hb == 1 and st3 == 2) else None)
                        t_oev = p.emit("scalar", lambda e, g=g, qc=qc, so=so: e.copy(
                            att[:, g * 512 + qc * 128: g * 512 + (qc + 1) * 128],
                            ps_O[so][:, :]), waits=[lastO], inc="act")
                        oro.release(so, t_oev)
                        prev_att_user = t_oev
                        last_pe3 = lastO
                        rotP.release(pP, t_ptev)
                        rotP.release(pP, lastO)
                        rotP.release(pP, t_poolS)
                        rotP.release(pP, t_oev)

                    for g in range(NP):
                        ws, wwt = wpr.acquire()
                        p.dma("sync", f"sC{ws}", wpad[ws][0:64, 0:640],
                              k_sp[g, 0:64, c * 512:c * 512 + 640],
                              waits=t_spk + wwt + [t_z1 if ws == 0 else t_z2])
                        p.dma("sync", f"sC{ws}", wpad[ws][64:128, 640:1280],
                              k_sp[g, 64:128, c * 512:c * 512 + 640])
                        t_wp = (f"sC{ws}", p.cnt[f"sC{ws}"], "g", True)
                        for qc in range(4):
                            tc = c * 4 + qc
                            sS, sw = rotS.acquire()
                            pP, pw = rotP.acquire()
                            lcol = lcol_[pP]
                            gcol = gcol_[pP]
                            lq = q_sb[:, g * 512 + qc * 128: g * 512 + (qc + 1) * 128]
                            rhsS = wpad[ws].rearrange("p (g2 w) -> p g2 w", g2=2)[
                                :, :, qc * 128:qc * 128 + 256]
                            p.emit("tensor", lambda e, sS=sS, lq=lq, rhsS=rhsS: e.matmul(
                                ps_S[sS][:, :], lq, rhsS, start=True, stop=False),
                                waits=[t_q, t_wp] + sw)
                            p.emit("tensor", lambda e, sS=sS, lq=lq, g=g: e.matmul(
                                ps_C[sS][:, :], lq,
                                ck_sb[:, g * 2 * NCK:(g + 1) * 2 * NCK],
                                start=True, stop=False), waits=[t_ckl])
                            slot = 0 if tc == 0 else 1
                            p.emit("tensor", lambda e, sS=sS, slot=slot: e.matmul(
                                ps_S[sS][:, :], ident[:, :],
                                lmask[:, slot * 512:(slot + 1) * 512],
                                start=False, stop=True), waits=[t_qc, t_id])
                            t_mm = p.emit("tensor", lambda e, sS=sS, tc=tc: e.matmul(
                                ps_C[sS][:, :], ident[:, :],
                                cmask[:, tc * 136:(tc + 1) * 136],
                                start=False, stop=True), inc="pe")
                            t_nm = p.emit("vector", lambda e, sS=sS, gcol=gcol: e.reduce_max(
                                gcol[:, 0:2],
                                ps_S[sS][:].rearrange("p (h n) -> p h n", h=2),
                                AX.X, negate=True),
                                waits=[t_mm] + sw + pw, inc="dve")
                            t_nc2 = p.emit("vector", lambda e, sS=sS, gcol=gcol: e.reduce_max(
                                gcol[:, 2:4],
                                ps_C[sS][:].rearrange("p (h n) -> p h n", h=2),
                                AX.X, negate=True), inc="dve")
                            ee = []
                            for hb in range(2):
                                ee.append(p.emit("scalar", lambda e, sS=sS, pP=pP, hb=hb, gcol=gcol, lcol=lcol: e.activation(
                                    P_sb[pP][:, hb * 256:(hb + 1) * 256],
                                    ps_S[sS][:, hb * 256:(hb + 1) * 256], AF.Exp,
                                    bias=gcol[:, hb:hb + 1],
                                    accum_out=lcol[:, hb:hb + 1]),
                                    waits=[t_nm, t_nc2], inc="act"))
                            for hb in range(2):
                                ee.append(p.emit("scalar", lambda e, sS=sS, pP=pP, hb=hb, gcol=gcol, lcol=lcol: e.activation(
                                    P_sb[pP][:, 512 + hb * 68:512 + (hb + 1) * 68],
                                    ps_C[sS][:, hb * 68:(hb + 1) * 68], AF.Exp,
                                    bias=gcol[:, 2 + hb:3 + hb],
                                    accum_out=lcol[:, 2 + hb:3 + hb]), inc="act"))
                            t_exp = ee[-1]
                            rotS.release(sS, ee[1])
                            rotS.release(sS, t_nm)
                            rotS.release(sS, t_nc2)
                            rotS.release(sS, ee[3])
                            pendq.append((pP, g, qc, tc, t_exp))
                            if len(pendq) > 2:
                                emit_tail(pendq.pop(0))
                        wpr.release(ws, last_pe3)
                    while pendq:
                        emit_tail(pendq.pop(0))
                    prev_q_user = last_pe3
                    prev_v_user = last_pe3
                    for dh2 in range(2):
                        t_xr = None
                        for tc4 in range(4):
                            t_xr = p.dma("gpsimd", "sA0",
                                         xres3[:, tc4 * 1024:(tc4 + 1) * 1024],
                                         x_in[E + c * 512 + tc4 * 128:
                                              E + c * 512 + (tc4 + 1) * 128,
                                              dh2 * 1024:(dh2 + 1) * 1024],
                                         waits=[prev_x1_user[0]] if tc4 == 0 else [])
                        t_xr = ("sA0", p.cnt["sA0"], "g", True)
                        for mh in range(8):
                            m = dh2 * 8 + mh
                            lastW = None
                            for kc in range(NF):
                                lastW = p.emit("tensor", lambda e, kc=kc, m=m: e.matmul(
                                    ps_W[:, :], wout_sb[kc][:, m * 128:(m + 1) * 128],
                                    att[:, kc * 512:(kc + 1) * 512],
                                    start=(kc == 0), stop=(kc == NF - 1)),
                                    waits=([t_wout, prev_att_user, prev_wevac]
                                           if kc == 0 else []),
                                    inc="pe" if kc == NF - 1 else None)
                            ys, ysw = yrot.acquire()
                            t_ye = p.emit("vector", lambda e, ys=ys: e.tensor_copy(
                                y3_sb[ys][:, :], ps_W[:, :]), waits=[lastW] + ysw,
                                inc="dve")
                            prev_wevac = t_ye
                            for tc4 in range(4):
                                ts2, tsw = oro.acquire()
                                t_yt = p.emit("tensor", lambda e, ys=ys, ts2=ts2, tc4=tc4: e.transpose(
                                    ps_O[ts2][:, :],
                                    y3_sb[ys][:, tc4 * 128:(tc4 + 1) * 128],
                                    ident32[:, :]), waits=[t_ye] + tsw, inc="pe")
                                t_ya = p.emit("vector", lambda e, ts2=ts2, tc4=tc4, mh=mh: e.tensor_add(
                                    x1st3[:, tc4 * 1024 + mh * 128:
                                          tc4 * 1024 + (mh + 1) * 128],
                                    ps_O[ts2][:, :],
                                    xres3[:, tc4 * 1024 + mh * 128:
                                          tc4 * 1024 + (mh + 1) * 128]),
                                    waits=[t_yt, t_xr] + prev_store[0], inc="dve")
                                prev_store[0] = []
                                oro.release(ts2, t_ya)
                                yrot.release(ys, t_ya)
                                prev_x1_user[0] = t_ya
                        stt3 = []
                        for tc4 in range(4):
                            p.dma("gpsimd", "out0",
                                  x1_sp[c * 512 + tc4 * 128: c * 512 + (tc4 + 1) * 128,
                                        dh2 * 1024:(dh2 + 1) * 1024],
                                  x1st3[:, tc4 * 1024:(tc4 + 1) * 1024],
                                  waits=[prev_x1_user[0]])
                        prev_store[0] = [("out0", p.cnt["out0"], "g", True)]

                t_x1w = ("out0", p.cnt["out0"], "g", True)
                with nc.Block() as blk:
                    p.replay(blk, sems)

            ph13.close()

            NG = 2
            GT = 1024
            NTC4 = GT // 128
            ph4o = contextlib.ExitStack()
            sb4o = lambda name, shape, dt=BF16: ph4o.enter_context(nc.sbuf_tensor(name, list(shape), dt))
            with ph4o:
                hdn = sb4o("hdn", (128, NI * GT))
                for grp4 in range(NG):
                    tok0 = grp4 * GT
                    ph4 = contextlib.ExitStack()
                    sb4 = lambda name, shape, dt=BF16: ph4.enter_context(nc.sbuf_tensor(name, list(shape), dt))
                    ps4 = lambda name, shape, dt=F32: ph4.enter_context(nc.psum_tensor(name, list(shape), dt))
                    with ph4:
                        p.set_barrier(p.engine_totals() +
                                      [("out0", p.cnt.get("out0", 0), "g", True)])
                        x4 = [sb4(f"x4_{grp4}_{i}", (128, DIM), F32) for i in range(2)]
                        sq4 = sb4(f"sq4_{grp4}", (128, DIM), F32)
                        xf_tm = [sb4(f"xftm{grp4}_{i}", (128, DIM)) for i in range(2)]
                        xf = sb4(f"xf_{grp4}", (128, NF * GT))
                        gsl = [sb4(f"gsl{grp4}_{i}", (128, NF * 384)) for i in range(2)]
                        usl = [sb4(f"usl{grp4}_{i}", (128, NF * 384)) for i in range(2)]
                        sil = sb4(f"sil_{grp4}", (128, 1024), F32)
                        s4a = sb4(f"s4a_{grp4}", (128, 16), F32)
                        s4b = sb4(f"s4b_{grp4}", (128, 16), F32)
                        s4c = sb4(f"s4c_{grp4}", (128, 16), F32)
                        s4d = sb4(f"s4d_{grp4}", (128, 16), F32)
                        s4e = sb4(f"s4e_{grp4}", (128, 16), F32)
                        ps_g = [ps4(f"ps_g{grp4}_{i}", (128, 512)) for i in range(2)]
                        ps_u = [ps4(f"ps_u{grp4}_{i}", (128, 512)) for i in range(2)]
                        ps_x = [ps4(f"ps_x{grp4}_{i}", (128, 128), BF16) for i in range(2)]

                        x4rot = Rot(2)
                        for tc in range(NTC4):
                            s, wt = x4rot.acquire()
                            t_x = p.dma("sync", f"qx{s}", x4[s][:],
                                        x1_sp[tok0 + tc * 128: tok0 + (tc + 1) * 128, :],
                                        waits=wt + [t_x1w])
                            v = "vector"
                            p.emit(v, lambda e, s=s: e.tensor_mul(sq4[:], x4[s][:], x4[s][:]),
                                   waits=[t_x])
                            p.emit(v, lambda e, s=s, tc=tc: e.reduce_sum(
                                s4a[:, tc:tc + 1], x4[s][:], AX.X))
                            t_r = p.emit(v, lambda e, tc=tc: e.reduce_sum(
                                s4b[:, tc:tc + 1], sq4[:], AX.X), inc="dve")
                            x4rot.release(s, t_r)
                        v = "vector"
                        SP4 = lambda: p.emit(v, lambda e: e.tensor_copy(s4e[:, 15:16],
                                                                        s4a[:, 0:1]))
                        SP4(); SP4(); SP4()
                        p.emit(v, lambda e: e.tensor_scalar_mul(s4c[:, 0:8], s4a[:, 0:8], 1.0 / DIM))
                        p.emit(v, lambda e: e.tensor_scalar_mul(s4d[:, 0:8], s4b[:, 0:8], 1.0 / DIM))
                        SP4(); SP4()
                        p.emit(v, lambda e: e.tensor_mul(s4e[:, 0:8], s4c[:, 0:8], s4c[:, 0:8]))
                        SP4(); SP4()
                        t_var4 = p.emit(v, lambda e: e.tensor_sub(s4d[:, 0:8], s4d[:, 0:8], s4e[:, 0:8]),
                                        inc="dve")
                        t_sq4 = p.emit("scalar", lambda e: e.activation(
                            s4e[:, 0:8], s4d[:, 0:8], AF.Sqrt, bias=eps4_c[:, :]),
                            waits=[t_var4, t_ep4], inc="act")
                        p.emit(v, lambda e: e.reciprocal(s4d[:, 8:16], s4e[:, 0:8]), waits=[t_sq4])
                        SP4(); SP4(); SP4()
                        p.emit(v, lambda e: e.tensor_mul(s4c[:, 8:16], s4c[:, 0:8], s4d[:, 8:16]))
                        SP4(); SP4()
                        t_st4 = p.emit(v, lambda e: e.tensor_scalar_mul(
                            s4c[:, 8:16], s4c[:, 8:16], -1.0), inc="dve")
                        SP4(); SP4(); SP4()
                        xtra = Rot(2)
                        xf_done = None
                        xh_rd = [None, None]
                        for tc in range(NTC4):
                            s, wt = x4rot.acquire()
                            t_x = p.dma("sync", f"qx{s}", x4[s][:],
                                        x1_sp[tok0 + tc * 128: tok0 + (tc + 1) * 128, :],
                                        waits=wt)
                            hs = tc % 2
                            t_xh = p.emit(v, lambda e, s=s, hs=hs, tc=tc: e.tensor_scalar(
                                xf_tm[hs][:], x4[s][:], s4d[:, 8 + tc:9 + tc],
                                s4c[:, 8 + tc:9 + tc], ALU.mult, ALU.add),
                                waits=[t_x, t_st4, xh_rd[hs]],
                                inc="dve")
                            x4rot.release(s, t_xh)
                            for f in range(NF):
                                ts, twt = xtra.acquire()
                                t_tp = p.emit("tensor", lambda e, f=f, hs=hs, ts=ts: e.transpose(
                                    ps_x[ts][:, :], xf_tm[hs][:, f * 128:(f + 1) * 128],
                                    ident[:, :]), waits=twt + [t_xh, t_id], inc="pe")
                                if f == NF - 1:
                                    xh_rd[hs] = t_tp
                                t_ev = p.emit(v, lambda e, f=f, tc=tc, ts=ts: e.tensor_copy(
                                    xf[:, f * GT + tc * 128: f * GT + (tc + 1) * 128],
                                    ps_x[ts][:, :]), waits=[t_tp], inc="dve")
                                xtra.release(ts, t_ev)
                                if f == NF - 1:
                                    xf_done = t_ev
                        NMG = 15
                        grot = Rot(2)
                        guguard = [[], []]
                        hdn_done = None
                        for mg in range(NMG):
                            mi0 = mg * 3
                            nmi = 3 if mg < 14 else 2
                            gs4, gw4 = grot.acquire()
                            for kh in range(2):
                                p.dma("sync", f"kc{gs4}",
                                      gsl[gs4][:].rearrange("p (k m) -> p k m", k=NF)[
                                          :, kh * 8:(kh + 1) * 8, 0:nmi * 128],
                                      gate_bf.rearrange("(k p) i -> p k i", p=128)[
                                          :, kh * 8:(kh + 1) * 8,
                                          mi0 * 128:(mi0 + nmi) * 128],
                                      waits=(gw4 + [t_wbf]) if kh == 0 else [])
                            for kh in range(2):
                                p.dma("sync", f"kc{gs4}",
                                      usl[gs4][:].rearrange("p (k m) -> p k m", k=NF)[
                                          :, kh * 8:(kh + 1) * 8, 0:nmi * 128],
                                      up_bf.rearrange("(k p) i -> p k i", p=128)[
                                          :, kh * 8:(kh + 1) * 8,
                                          mi0 * 128:(mi0 + nmi) * 128])
                            t_gu = (f"kc{gs4}", p.cnt[f"kc{gs4}"], "g", True)
                            lastgu = None
                            for mi in range(nmi):
                                m = mi0 + mi
                                for n2 in range(2):
                                    pgs = (m * 2 + n2) % 2
                                    gw9 = guguard[pgs]
                                    guguard[pgs] = []
                                    for which, slab, pp4 in ((0, gsl, ps_g), (1, usl, ps_u)):
                                        for kc in range(NF):
                                            lastgu = p.emit("tensor", lambda e, kc=kc, mi=mi, n2=n2, which=which, gs4=gs4, pgs=pgs, slab=slab, pp4=pp4: e.matmul(
                                                pp4[pgs][:, :],
                                                slab[gs4][:, kc * 384 + mi * 128:
                                                          kc * 384 + (mi + 1) * 128],
                                                xf[:, kc * GT + n2 * 512:
                                                   kc * GT + (n2 + 1) * 512],
                                                start=(kc == 0), stop=(kc == NF - 1)),
                                                waits=([t_gu, xf_done] + gw9
                                                       if kc == 0 else []),
                                                inc="pe" if kc == NF - 1 else None)
                                    t_si = p.emit("scalar", lambda e, pgs=pgs, m=m: e.activation(
                                        sil[:, pgs * 512:(pgs + 1) * 512], ps_g[pgs][:, :],
                                        AF.Silu,
                                        scale=scales_g[:, m:m + 1]), waits=[lastgu],
                                        inc="act")
                                    t_hd = p.emit("vector", lambda e, pgs=pgs, m=m, n2=n2: e.scalar_tensor_tensor(
                                        hdn[:, m * GT + n2 * 512: m * GT + (n2 + 1) * 512],
                                        ps_u[pgs][:, :], scales_u[:, m:m + 1],
                                        sil[:, pgs * 512:(pgs + 1) * 512],
                                        ALU.mult, ALU.mult), waits=[t_si], inc="dve")
                                    guguard[pgs] = [t_si, t_hd]
                                    hdn_done = t_hd
                            grot.release(gs4, lastgu)
                        t_p4a = [("pe", p.cnt["pe"], "n", True),
                                 ("dve", p.cnt["dve"], "n", True),
                                 ("act", p.cnt["act"], "n", True)]
                        with nc.Block() as blk:
                            p.replay(blk, sems)
                    ph4b = contextlib.ExitStack()
                    sb4b = lambda name, shape, dt=BF16: ph4b.enter_context(nc.sbuf_tensor(name, list(shape), dt))
                    ps4b = lambda name, shape, dt=F32: ph4b.enter_context(nc.psum_tensor(name, list(shape), dt))
                    with ph4b:
                        p.set_barrier(p.engine_totals() +
                                      [("out0", p.cnt.get("out0", 0), "g", True)])
                        dsl = [sb4b(f"dsl{grp4}_{i}", (128, NI * 128)) for i in range(2)]
                        y4 = [sb4b(f"y4_{grp4}_{i}", (128, 1024), F32) for i in range(2)]
                        xr4 = sb4b(f"xr4_{grp4}", (128, NTC4 * 512), F32)
                        o4 = sb4b(f"o4_{grp4}", (128, NTC4 * 512), F32)
                        ps_d = [ps4b(f"ps_d{grp4}_{i}", (128, 512)) for i in range(2)]
                        ps_y = [ps4b(f"ps_y{grp4}_{i}", (128, 128)) for i in range(2)]
                        drot = Rot(2)
                        y4r = Rot(2)
                        y4t = Rot(2)
                        dguard = [[], []]
                        pend4 = None
                        prev_o4 = [t_p4a[1], t_x1w]
                        prev_ost = []

                        def emit_tail4(it):
                            nonlocal prev_o4, prev_ost
                            ys4, m, mh4, t_ye4, _txr = it
                            for tc8 in range(8):
                                ts4, tsw4 = y4t.acquire()
                                t_yt4 = p.emit("tensor", lambda e, ys4=ys4, ts4=ts4, tc8=tc8: e.transpose(
                                    ps_y[ts4][:, :],
                                    y4[ys4][:, (tc8 % 4) * 128:(tc8 % 4 + 1) * 128]
                                    if False else
                                    y4[ys4][:, tc8 * 128:(tc8 + 1) * 128],
                                    ident32[:, :]), waits=[t_ye4] + tsw4, inc="pe")
                                t_ya4 = p.emit("vector", lambda e, ts4=ts4, tc8=tc8, mh4=mh4: e.tensor_add(
                                    o4[:, tc8 * 512 + mh4 * 128:
                                       tc8 * 512 + (mh4 + 1) * 128],
                                    ps_y[ts4][:, :],
                                    xr4[:, tc8 * 512 + mh4 * 128:
                                        tc8 * 512 + (mh4 + 1) * 128]),
                                    waits=[t_yt4, _txr] + prev_ost, inc="dve")
                                prev_ost = []
                                y4t.release(ts4, t_ya4)
                                y4r.release(ys4, t_ya4)
                                prev_o4 = [t_ya4]
                        for mq in range(4):
                            t_xr4 = None
                            for tc4 in range(NTC4):
                                t_xr4 = p.dma("gpsimd", "sA1",
                                              xr4[:, tc4 * 512:(tc4 + 1) * 512],
                                              x1_sp[tok0 + tc4 * 128: tok0 + (tc4 + 1) * 128,
                                                    mq * 512:(mq + 1) * 512],
                                              waits=(prev_o4 if tc4 == 0 else []))
                            t_xr4 = ("sA1", p.cnt["sA1"], "g", True)
                            prev_o4 = []
                            for mh4 in range(4):
                                m = mq * 4 + mh4
                                ds4, dw4 = drot.acquire()
                                p.dma("sync", f"kx{ds4}",
                                      dsl[ds4][:].rearrange("p (k m) -> p k m", k=NI),
                                      down_bf.rearrange("(k p) d -> p k d", p=128)[
                                          :, :, m * 128:(m + 1) * 128],
                                      waits=dw4 + [t_wbf])
                                t_ds = (f"kx{ds4}", p.cnt[f"kx{ds4}"], "g", True)
                                ys4, ysw4 = y4r.acquire()
                                t_last_ev = None
                                for n2 in range(2):
                                    pds = n2
                                    dw9 = dguard[pds]
                                    dguard[pds] = []
                                    lastd = None
                                    for kc in range(NI):
                                        lastd = p.emit("tensor", lambda e, kc=kc, ds4=ds4, pds=pds, n2=n2: e.matmul(
                                            ps_d[pds][:, :],
                                            dsl[ds4][:, kc * 128:(kc + 1) * 128],
                                            hdn[:, kc * GT + n2 * 512:
                                                kc * GT + (n2 + 1) * 512],
                                            start=(kc == 0), stop=(kc == NI - 1)),
                                            waits=([t_ds, hdn_done] + dw9 + (ysw4 if n2 == 0 else [])
                                                   if kc == 0 else []),
                                            inc="pe" if kc == NI - 1 else None)
                                    t_ye4 = p.emit("vector", lambda e, ys4=ys4, pds=pds, m=m, n2=n2: e.tensor_scalar_mul(
                                        y4[ys4][:, n2 * 512:(n2 + 1) * 512],
                                        ps_d[pds][:, :],
                                        scales_d[:, m:m + 1]), waits=[lastd],
                                        inc="dve")
                                    dguard[pds] = [t_ye4]
                                    t_last_ev = t_ye4
                                if pend4 is not None:
                                    emit_tail4(pend4)
                                pend4 = (ys4, m, mh4, t_last_ev, t_xr4)
                                drot.release(ds4, lastd)
                            if pend4 is not None:
                                emit_tail4(pend4)
                                pend4 = None
                            for tc4 in range(NTC4):
                                p.dma("gpsimd", "out1",
                                      out_h[tok0 + tc4 * 128: tok0 + (tc4 + 1) * 128,
                                            mq * 512:(mq + 1) * 512],
                                      o4[:, tc4 * 512:(tc4 + 1) * 512],
                                      waits=[prev_o4[0]])
                            prev_ost = [("out1", p.cnt["out1"], "g", True)]
                        with nc.Block() as blk:
                            p.replay(blk, sems)

            def final_wait(e):
                for s, v in p.cnt.items():
                    e.wait_ge(sems[s], v)
            with nc.Block() as blk:
                blk.gpsimd(final_wait)

    return nc


def _prep_shared(inp):
    f32 = np.float32
    d = {}
    d["wqkv"] = np.ascontiguousarray(inp["L0_attn_w_qkv"], f32)
    d["rmsw_c"] = np.ascontiguousarray(inp["L0_attn_rms_w"].reshape(NF, 128).T, f32)
    d["wcomb"] = np.ascontiguousarray(inp["L0_attn_w_comb"], f32)
    d["bcomb"] = np.ascontiguousarray(inp["L0_attn_b_comb"].reshape(1, 96), f32)
    kc, vc = inp["L0_attn_kc_w"], inp["L0_attn_vc_w"]

    def pair_itow(w):
        ev = w[0::2].transpose(0, 2, 3, 1)
        od = w[1::2].transpose(0, 2, 3, 1)
        return np.concatenate([ev, od], axis=1)
    d["kcw_l"] = np.ascontiguousarray(
        np.stack([pair_itow(kc), pair_itow(vc)], axis=1), f32)
    kp, vp = inp["L0_attn_k_pos"], inp["L0_attn_v_pos"]

    def pair_pos(w):
        ev = w[0::2].transpose(0, 2, 1)
        od = w[1::2].transpose(0, 2, 1)
        return np.concatenate([ev, od], axis=1)
    d["kpos_l"] = np.ascontiguousarray(
        np.stack([pair_pos(kp), pair_pos(vp)], axis=1), f32)
    kb, vb = inp["L0_attn_kc_b"], inp["L0_attn_vc_b"]

    def pair_b(w):
        return np.concatenate([w[0::2], w[1::2]], axis=1).T
    d["kcb_l"] = np.ascontiguousarray(
        np.concatenate([pair_b(kb), pair_b(vb)], axis=1), f32)
    mm = inp["L0_attn_mem_kv"]

    def pair_m(w):
        ev = w[0::2].transpose(0, 2, 1)
        od = w[1::2].transpose(0, 2, 1)
        return np.concatenate([ev, od], axis=1)
    d["mem_l"] = np.ascontiguousarray(np.stack([pair_m(mm[0]), pair_m(mm[1])]), f32)
    d["wout"] = np.ascontiguousarray(inp["L0_attn_w_out"], f32)
    d["gateT"] = np.ascontiguousarray(inp["L0_ffn_gate_w"].T, f32)
    d["upT"] = np.ascontiguousarray(inp["L0_ffn_up_w"].T, f32)
    d["downT"] = np.ascontiguousarray(inp["L0_ffn_down_w"].T, f32)
    d["gs_c"] = np.ascontiguousarray(inp["L0_ffn_gate_s"].reshape(NI, 128).T, f32)
    d["us_c"] = np.ascontiguousarray(inp["L0_ffn_up_s"].reshape(NI, 128).T, f32)
    d["ds_c"] = np.ascontiguousarray(inp["L0_ffn_down_s"].reshape(NF, 128).T, f32)
    d["ident_h"] = np.eye(128, dtype=BF)
    d["ident32_h"] = np.eye(128, dtype=np.float32)
    return d


def _masks(h):
    i = np.arange(128)[:, None]
    j = np.arange(NKK)[None, :]
    std = (i <= j) & (j <= i + 128)
    first = std & ((j >= 128) if h == 0 else True)
    lm = np.full((2, 128, NKK), NEG, np.float32)
    lm[0][first] = 0.0
    lm[1][std] = 0.0
    cm = np.full((16, 128, NCK), NEG, np.float32)
    for tc in range(16):
        ig = h * T + tc * 128 + np.arange(128)[:, None]
        g = np.arange(NCK)[None, :] - 4
        ok = (g >= 0) & (g >= ig // CBS)
        cm[tc][np.broadcast_to(ok, (128, NCK))] = 0.0
    lm2 = np.concatenate([lm, lm], axis=2)
    cm2 = np.concatenate([cm, cm], axis=2)
    return lm2.astype(BF), cm2.astype(BF)


def make_in_maps(inp):
    shared = _prep_shared(inp)
    x = np.asarray(inp["L0_attn_inp"], np.float32)
    maps = []
    for c in range(8):
        b, h = c // 2, c % 2
        xs = np.zeros((RN, DIM), np.float32)
        xs[E:] = x[b, h * T:(h + 1) * T]
        if h == 1:
            xs[:E] = x[b, T - E:T]
        lm, cm = _masks(h)
        m = dict(shared)
        m["x_in"] = xs
        m["lmask_h"] = lm
        m["cmask_h"] = cm
        maps.append(m)
    return maps


def kernel(**inp):
    nc = build_nc(debug=0)
    maps = make_in_maps(inp)
    res = run_bass_kernel_spmd(nc, maps, core_ids=list(range(8)))
    out = np.empty((4, 4096, DIM), np.float32)
    for c in range(8):
        b, h = c // 2, c % 2
        out[b, h * T:(h + 1) * T] = res.results[c]["out"]
    return out
